# version 14 (final)
# baseline (speedup 1.0000x reference)
import numpy as np
from contextlib import ExitStack

import concourse.bass as bass
import concourse.mybir as mybir
from concourse.bass_utils import run_bass_kernel_spmd

F32 = mybir.dt.float32
BF16 = mybir.dt.bfloat16
FP16 = mybir.dt.float16
I32 = mybir.dt.int32
AF = mybir.ActivationFunctionType
ALU = mybir.AluOpType

ENGINES = ("pe", "act", "dve", "pool", "sp")


class Cfg:
    def __init__(self, D, I, T, PW, bi_iters=13, bi_lo=0.125, bi_hi=0.5):
        self.D = D
        self.I = I
        self.T = T
        self.PW = PW
        self.KD = D // 128
        self.ITI = I // 128
        self.NPAN = I // PW
        self.DF = D // 4
        self.KDF = self.DF // 128
        self.IF = I // 4
        self.KIF = (self.IF + 127) // 128
        self.IF_last = self.IF - (self.KIF - 1) * 128
        self.HT = D // 128
        self.NCH = 2
        self.TC = T // 2
        self.TT = self.TC // 128
        self.KK = I // 2
        self.BI_IT = bi_iters
        self.BI_LO = bi_lo
        self.BI_HI = bi_hi
        self.CSL = 8
        assert D % 512 == 0 and I % 256 == 0 and self.IF % 64 == 0
        assert I % PW == 0 and PW <= 512 and I % self.CSL == 0
        assert T % 256 == 0 and self.DF % 128 == 0


class Sch:

    def __init__(self, nc, es):
        self.nc = nc
        self.ops = {e: [] for e in ENGINES}
        self.sem = {}
        self.cnt = {}
        self.lastwait = {e: {} for e in ENGINES}
        self.es = es

    def mksem(self, *names):
        for n in names:
            self.sem[n] = self.es.enter_context(self.nc.semaphore(n))
            self.cnt[n] = 0

    def emit(self, eng, fn, waits=(), inc=None, n=1):
        if eng in ("act", "dve", "pool"):
            waits = list(waits) + [(eng, self.cnt[eng])]
        w = []
        for (s, v) in waits:
            if v > self.lastwait[eng].get(s, -1):
                self.lastwait[eng][s] = v
                w.append((s, v))
        val = None
        if inc is not None:
            self.cnt[inc] += n
            val = self.cnt[inc]
        self.ops[eng].append((tuple(w), fn, inc, n))
        return val

    def run(self, eng, h):
        for w, fn, inc, n in self.ops[eng]:
            for (s, v) in w:
                h.wait_ge(self.sem[s], v)
            ins = fn(h)
            if inc is not None:
                ins.then_inc(self.sem[inc], n)


def fold_segments(it, IF):
    segs = []
    p = 0
    while p < 128:
        f = (128 * it + p) % IF
        j, off = divmod(f, 128)
        run = min(128 - p, 128 - off, IF - f)
        assert off % 64 == 0 and run % 64 == 0 and p % 64 == 0
        segs.append((p, run, j, off))
        p += run
    return segs


def build_program(c: Cfg):
    nc = bass.Bass("TRN2", target_bir_lowering=False)

    xT = nc.dram_tensor("xT", [c.D, c.T], F32, kind="ExternalInput").ap()
    wg = nc.dram_tensor("wg", [c.NPAN, 128, c.KD * c.PW], F32, kind="ExternalInput").ap()
    wu = nc.dram_tensor("wu", [c.NPAN, 128, c.KD * c.PW], F32, kind="ExternalInput").ap()
    wd = nc.dram_tensor("wd", [c.HT, 128, c.ITI * 128], F32, kind="ExternalInput").ap()
    w13 = nc.dram_tensor("w13", [c.ITI, 128, 2 * (c.KDF + 1) * 128], F32, kind="ExternalInput").ap()
    w2 = nc.dram_tensor("w2", [c.HT, 128, (c.KIF + 1) * 128], F32, kind="ExternalInput").ap()
    a1 = nc.dram_tensor("a1", [c.KD, 128, 2], F32, kind="ExternalInput").ap()
    a3 = nc.dram_tensor("a3", [c.KD, 128, 2], F32, kind="ExternalInput").ap()
    a2 = nc.dram_tensor("a2", [c.ITI, 128, 2], F32, kind="ExternalInput").ap()
    outT = nc.dram_tensor("outT", [c.D, c.T], F32, kind="ExternalOutput").ap()
    pscr = nc.dram_tensor("pscr", [c.TT, 128, c.I], BF16, kind="Internal").ap()
    rscr = nc.dram_tensor("rscr", [c.HT, 128, c.TC], F32, kind="Internal").ap()

    xT_r = xT.rearrange("(k p) t -> p k t", p=128)
    pscr_r = pscr.rearrange("t p i -> p t i")

    CW = c.I // c.CSL
    PSA = max(c.TT * c.PW, CW, c.KDF * c.TC // 2)
    PSW = PSA + 4 * c.TT * 128
    W1B = (c.KDF + 1) * 128
    QB = 4
    W13B = 2 * W1B
    WPW = max(c.KD * c.PW, c.ITI * 128, (c.KIF + 1) * 128)
    W3S = (2 * WPW) // 3
    QF = max(1, W3S // W13B)
    QF = min(3, QF)
    assert QF * W13B <= W3S
    HFW = c.KIF * c.TC
    invD = 1.0 / c.D

    es = ExitStack()
    with es:
        ident_i = es.enter_context(nc.sbuf_tensor("ident_i", [128, 128], I32))
        ident = es.enter_context(nc.sbuf_tensor("ident", [128, 128], BF16))
        eplus = es.enter_context(nc.sbuf_tensor("eplus", [128, 128], BF16))
        eminus = es.enter_context(nc.sbuf_tensor("eminus", [128, 128], BF16))
        idlow = es.enter_context(nc.sbuf_tensor("idlow", [128, 128], BF16))
        xtb = es.enter_context(nc.sbuf_tensor("xtb", [128, c.KD * c.TC], BF16))
        wpan = es.enter_context(nc.sbuf_tensor("wpan", [128, 2, WPW], BF16))
        wpan3 = wpan[:].rearrange("p a b -> p (a b)")[:, :3 * W3S].rearrange("p (s w) -> p s w", s=3, w=W3S)
        arena_h = es.enter_context(nc.sbuf_tensor("arena", [128, c.TT * c.I], BF16))
        pstage = es.enter_context(nc.sbuf_tensor("pstage", [128, 2, PSW], BF16))
        sgt = es.enter_context(nc.sbuf_tensor("sgt", [128, c.TT, c.PW], F32))
        mrsb = es.enter_context(nc.sbuf_tensor("mrsb", [128, 2, c.TC], F32))
        SA = es.enter_context(nc.sbuf_tensor("SA", [128, 2, c.TC], F32))
        SB = es.enter_context(nc.sbuf_tensor("SB", [128, 2, c.TC], F32))
        SC = es.enter_context(nc.sbuf_tensor("SC", [128, 2, c.TC], F32))
        a1sbf = es.enter_context(nc.sbuf_tensor("a1sbf", [128, c.KD * 2], F32))
        a3sbf = es.enter_context(nc.sbuf_tensor("a3sbf", [128, c.KD * 2], F32))
        asums = es.enter_context(nc.sbuf_tensor("asums", [2, 2], F32))
        a2sb = es.enter_context(nc.sbuf_tensor("a2sb", [128, c.ITI * 2], BF16))
        onec = es.enter_context(nc.sbuf_tensor("onec", [128, 1], F32))
        oner = es.enter_context(nc.sbuf_tensor("oner", [1, 128], F32))
        lo = es.enter_context(nc.sbuf_tensor("lo", [128, c.TT], F32))
        hi = es.enter_context(nc.sbuf_tensor("hi", [128, c.TT], F32))
        mid = es.enter_context(nc.sbuf_tensor("mid", [128, c.TT], F32))
        cntb = es.enter_context(nc.sbuf_tensor("cntb", [128, c.TT], F32))
        cpart = es.enter_context(nc.sbuf_tensor("cpart", [128, c.TT * c.CSL], F32))
        predb = es.enter_context(nc.sbuf_tensor("predb", [128, c.TT], F32))
        bist = es.enter_context(nc.sbuf_tensor("bist", [128, c.TT], F32))
        negmid = es.enter_context(nc.sbuf_tensor("negmid", [128, c.TT], F32))
        tfin = es.enter_context(nc.sbuf_tensor("tfin", [128, c.TT], F32))
        tbig = es.enter_context(nc.sbuf_tensor("tbig", [128, c.TT * 128], F32))
        rstdb = es.enter_context(nc.sbuf_tensor("rstdb", [1, c.TC], F32))

        arena = arena_h[:]
        xff_v = xtb.bitcast(F32)[:]
        smt_a = SA.bitcast(BF16)[:].rearrange("p a b -> p (a b)").rearrange(
            "p (g w) -> p g w", g=4)
        smt_b = SB.bitcast(BF16)[:].rearrange("p a b -> p (a b)").rearrange(
            "p (g w) -> p g w", g=4)
        smt2 = (smt_a, smt_b)
        scb2 = pstage[:, :, PSA:]
        xa = pstage.rearrange("p a b -> p (a b)")[0:2, PSA:PSA + 3 * c.TC]
        meanb = tbig[0:1, :]
        SCr0 = SC.rearrange("p a b -> p (a b)")
        stA = SCr0[0:1, 0:c.TC]
        stB = SCr0[0:1, c.TC:2 * c.TC]
        arena_f16 = arena_h.bitcast(FP16)
        arena_f32 = arena_h.bitcast(F32)
        smTv = arena_h[:].rearrange("p (t i) -> p t i", t=c.TT)

        def xfbf(j):
            half = (j * c.TC) // PSA
            off = (j * c.TC) % PSA
            return pstage[:, half, off:off + c.TC]

        S = Sch(nc, es)
        S.mksem("pe", "act", "dve", "pool")
        S.mksem("dxt", "dwa", "dwb", "dpoa", "dpob", "dpia", "dpib",
                "dsa", "dsb", "dspa", "dspb", "douta", "doutb", "dmisc")
        SEM = S.sem
        stop_after = getattr(c, "stop_after", 99)

        def finish():
            with nc.Block() as blk:
                @blk.tensor
                def _(h):
                    S.run("pe", h)

                @blk.scalar
                def _(h):
                    S.run("act", h)

                @blk.vector
                def _(h):
                    S.run("dve", h)

                @blk.gpsimd
                def _(h):
                    S.run("pool", h)

                @blk.sync
                def _(h):
                    S.run("sp", h)

        def finish_early():
            mz = S.emit("dve", lambda h: h.memset(SB[:, 0, :], 0.0), inc="dve")
            st = S.emit("sp", lambda h: h.dma_start(outT[0:128, 0:c.TC], SB[:, 0, :]),
                        waits=[("dve", mz)], inc="douta", n=16)
            S.emit("sp", lambda h: h.nop(), waits=[("douta", st)])
            finish()

        S.emit("pool", lambda h: h.iota(ident_i[:], [[1, 128]], base=0, channel_multiplier=-1), inc="pool")
        S.emit("dve", lambda h: h.tensor_scalar(ident[:], ident_i[:], 0.0, None, op0=ALU.is_equal),
               waits=[("pool", S.cnt["pool"])], inc="dve")
        S.emit("pool", lambda h: h.dma_start(
            a1sbf[:].rearrange("p (k m) -> p k m", m=2), a1.rearrange("k p m -> p k m")), inc="dmisc", n=16)
        S.emit("pool", lambda h: h.dma_start(
            a3sbf[:].rearrange("p (k m) -> p k m", m=2), a3.rearrange("k p m -> p k m")), inc="dmisc", n=16)
        S.emit("pool", lambda h: h.dma_start(
            a2sb[:].rearrange("p (k m) -> p k m", m=2), a2.rearrange("k p m -> p k m")), inc="dmisc", n=16)
        S.emit("pool", lambda h: h.iota(ident_i[:], [[1, 128]], base=64, channel_multiplier=-1),
               waits=[("dve", S.cnt["dve"])], inc="pool")
        S.emit("dve", lambda h: h.tensor_scalar(eminus[:], ident_i[:], 0.0, None, op0=ALU.is_equal),
               waits=[("pool", S.cnt["pool"])], inc="dve")
        S.emit("pool", lambda h: h.iota(ident_i[:], [[1, 128]], base=-64, channel_multiplier=-1),
               waits=[("dve", S.cnt["dve"])], inc="pool")
        S.emit("dve", lambda h: h.tensor_scalar(eplus[:], ident_i[:], 0.0, None, op0=ALU.is_equal),
               waits=[("pool", S.cnt["pool"])], inc="dve")
        S.emit("dve", lambda h: h.tensor_copy(idlow[0:64, :], ident[0:64, :]), inc="dve")
        S.emit("dve", lambda h: h.memset(idlow[64:128, :], 0.0), inc="dve")
        S.emit("dve", lambda h: h.memset(eplus[64:128, :], 0.0), inc="dve")
        S.emit("dve", lambda h: h.memset(eminus[0:64, :], 0.0), inc="dve")
        S.emit("dve", lambda h: h.memset(onec[:], 1.0), inc="dve")
        S.emit("dve", lambda h: h.memset(oner[:], 1.0), inc="dve")
        misc_done = S.cnt["dmisc"]
        ident_dve = S.cnt["dve"]
        with nc.psum_tensor("asump", [2, 2], F32) as asump:
            def asmm(h):
                last = None
                for kc in range(c.KD):
                    last = h.matmul(asump[:, 0:1], a1sbf[:, kc * 2:(kc + 1) * 2], onec[:],
                                    start=(kc == 0), stop=(kc == c.KD - 1))
                for kc in range(c.KD):
                    last = h.matmul(asump[:, 1:2], a3sbf[:, kc * 2:(kc + 1) * 2], onec[:],
                                    start=(kc == 0), stop=(kc == c.KD - 1))
                return last
            as_pe = S.emit("pe", asmm,
                           waits=[("dmisc", misc_done), ("dve", ident_dve)], inc="pe")
            S.emit("dve", lambda h: h.tensor_copy(asums[:], asump[:]),
                   waits=[("pe", as_pe)], inc="dve")
        prologue_dve = S.cnt["dve"]

        pe_end_prev_chunk = 0
        dve_end_prev_chunk = 0
        xtb_pe_prev = 0
        ln2psum_prev = 0
        xtb_dve_prev = 0
        arena_pe_prev = 0

        for ch in range(c.NCH):
            t0 = ch * c.TC
            dout_prev = (S.cnt["douta"], S.cnt["doutb"])

            xtb_war = [("dve", prologue_dve)]
            if ch > 0:
                xtb_war = [("dve", xtb_dve_prev)]
            xt_ld = S.emit("pool", lambda h, t0=t0: h.dma_start(
                xtb[:].rearrange("p (k t) -> p k t", k=c.KD),
                xT_r[:, :, t0:t0 + c.TC]), waits=xtb_war, inc="dxt", n=16)

            with nc.psum_tensor(f"gps{ch}", [128, c.TT * 512], F32) as gps, \
                 nc.psum_tensor(f"ups{ch}", [128, c.TT * 512], F32) as ups:
                psA = [gps, ups]
                g_done = [0] * c.NPAN
                u_done = [0] * c.NPAN
                silu_done = {}
                sgc_done = {}
                pm_done = {}
                pout_id = [0] * c.NPAN
                pout2_id = [0] * c.NPAN
                arena_war = [("pe", arena_pe_prev)] if ch > 0 else []

                for n in range(c.NPAN):
                    hbuf = n % 2
                    dws = "dwa" if hbuf == 0 else "dwb"
                    dpo = "dpoa" if hbuf == 0 else "dpob"
                    gw = ([("pe", g_done[n - 2])] if n >= 2 else
                          [("pe", xtb_pe_prev), ("dve", ln2psum_prev)])
                    gld = S.emit("pool", lambda h, n=n, hbuf=hbuf: h.dma_start(
                        wpan[:, hbuf, :c.KD * c.PW], wg[n]), waits=gw, inc=dws, n=16)

                    def gmm(h, hbuf=hbuf):
                        last = None
                        for t in range(c.TT):
                            for kc in range(c.KD):
                                last = h.matmul(
                                    psA[hbuf][:, t * 512:t * 512 + c.PW],
                                    xtb[:, kc * c.TC + t * 128: kc * c.TC + (t + 1) * 128],
                                    wpan[:, hbuf, kc * c.PW:(kc + 1) * c.PW],
                                    start=(kc == 0), stop=(kc == c.KD - 1))
                        return last
                    gwaits = [(dws, gld), ("dxt", xt_ld)]
                    if n >= 2:
                        gwaits.append(("act", silu_done[(n - 2, c.TT - 1)]))
                    g_done[n] = S.emit("pe", gmm, waits=gwaits, inc="pe")

                    for t in range(c.TT):
                        sw = [("pe", g_done[n])]
                        if n >= 1:
                            sw.append(("dve", sgc_done.get((n - 1, t), 0)))
                        silu_done[(n, t)] = S.emit(
                            "act", lambda h, t=t, hbuf=hbuf: h.activation(
                                sgt[:, t, :], psA[hbuf][:, t * 512:t * 512 + c.PW], AF.Silu),
                            waits=sw, inc="act")
                        S.emit("act", lambda h, n=n, t=t: h.activation(
                            arena_f16[:, t * c.I + n * c.PW: t * c.I + (n + 1) * c.PW],
                            sgt[:, t, :], AF.Abs),
                            waits=arena_war if n == 0 else (), inc="act")
                        cw_ = [("act", silu_done[(n, t)])]
                        if n >= 2:
                            cw_.append((dpo, pout_id[n - 2]))
                        sgc_done[(n, t)] = S.emit("dve", lambda h, n=n, t=t: h.tensor_copy(
                            pstage[:, n % 2, t * c.PW:(t + 1) * c.PW], sgt[:, t, :]),
                            waits=cw_, inc="dve")
                    pout_id[n] = S.emit("sp", lambda h, n=n: h.dma_start(
                        pscr_r[:, :, n * c.PW:(n + 1) * c.PW],
                        pstage[:, n % 2, :c.TT * c.PW].rearrange("p (t w) -> p t w", t=c.TT)),
                        waits=[("dve", sgc_done[(n, c.TT - 1)])], inc=dpo, n=16)

                A1_act = S.cnt["act"]
                A1_dve = S.cnt["dve"]
                phaseA_pe = None

                SBb = SB.bitcast(BF16)[:].rearrange("p a b -> p (a b)")
                SCb = SC.bitcast(BF16)[:].rearrange("p a b -> p (a b)")
                cpv = cpart[:].rearrange("p (t s) -> p s t", s=c.CSL)
                NDV = c.CSL // 2
                EPS = 2.0 ** -20
                jw = [("douta", dout_prev[0]), ("doutb", dout_prev[1])]
                S.emit("dve", lambda h: h.memset(lo[:], c.BI_LO), waits=jw, inc="dve")
                S.emit("dve", lambda h: h.memset(hi[:], c.BI_HI),
                       waits=[("act", A1_act)], inc="dve")

                def count_iter(it_bi):
                    S.emit("dve", lambda h: h.tensor_add(mid[:], lo[:], hi[:]), inc="dve")
                    S.emit("dve", lambda h: h.tensor_scalar_mul(mid[:], mid[:], 0.5), inc="dve")
                    nm_dve = S.emit("dve", lambda h: h.tensor_scalar(
                        negmid[:], mid[:], -1.0, -EPS, op0=ALU.mult, op1=ALU.add), inc="dve")
                    act_cnt = None
                    for t in range(c.TT):
                        for sl in range(c.CSL):
                            if sl < NDV:
                                S.emit("dve", lambda h, t=t, sl=sl: h.tensor_scalar(
                                    SBb[:, 0:CW],
                                    arena_f16[:, t * c.I + sl * CW: t * c.I + (sl + 1) * CW],
                                    mid[:, t:t + 1], None, op0=ALU.is_gt, op1=ALU.add,
                                    accum_out=cpart[:, t * c.CSL + sl: t * c.CSL + sl + 1]),
                                    inc="dve")
                            else:
                                act_cnt = S.emit("act", lambda h, t=t, sl=sl: h.activation(
                                    SCb[:, 0:CW],
                                    arena_f16[:, t * c.I + sl * CW: t * c.I + (sl + 1) * CW],
                                    AF.Sign, bias=negmid[:, t:t + 1], scale=1.0,
                                    accum_out=cpart[:, t * c.CSL + sl: t * c.CSL + sl + 1]),
                                    waits=[("dve", nm_dve)] + jw, inc="act")
                    S.emit("dve", lambda h: h.tensor_add(cntb[:], cpv[:, NDV, :], cpv[:, NDV + 1, :]),
                           waits=[("act", act_cnt)], inc="dve")
                    for sl in range(NDV + 2, c.CSL):
                        S.emit("dve", lambda h, sl=sl: h.tensor_add(cntb[:], cntb[:], cpv[:, sl, :]), inc="dve")
                    S.emit("dve", lambda h: h.tensor_scalar(
                        cntb[:], cntb[:], float((c.CSL - NDV) * CW), 0.5,
                        op0=ALU.add, op1=ALU.mult), inc="dve")
                    for sl in range(NDV):
                        S.emit("dve", lambda h, sl=sl: h.tensor_add(cntb[:], cntb[:], cpv[:, sl, :]), inc="dve")
                    S.emit("dve", lambda h: h.tensor_scalar(
                        predb[:], cntb[:], float(c.KK), None, op0=ALU.is_ge), inc="dve")
                    S.emit("dve", lambda h: h.tensor_sub(bist[:], mid[:], lo[:]), inc="dve")
                    S.emit("dve", lambda h: h.tensor_mul(bist[:], bist[:], predb[:]), inc="dve")
                    S.emit("dve", lambda h: h.tensor_add(lo[:], lo[:], bist[:]), inc="dve")
                    S.emit("dve", lambda h: h.tensor_sub(bist[:], hi[:], mid[:]), inc="dve")
                    S.emit("dve", lambda h: h.tensor_mul(bist[:], bist[:], predb[:]), inc="dve")
                    S.emit("dve", lambda h: h.tensor_add(hi[:], mid[:], bist[:]), inc="dve")

                def a2_panel(n):
                    hbuf = n % 2
                    dws = "dwa" if hbuf == 0 else "dwb"
                    dpo = "dpoa" if hbuf == 0 else "dpob"
                    dpi = "dpia" if hbuf == 0 else "dpib"
                    uw = [("pe", u_done[n - 2])] if n >= 2 else [("pe", g_done[c.NPAN - 1])]
                    uld = S.emit("pool", lambda h, n=n, hbuf=hbuf: h.dma_start(
                        wpan[:, hbuf, :c.KD * c.PW], wu[n]), waits=uw, inc=dws, n=16)
                    siw = [(dpo, pout_id[c.NPAN - 2 + hbuf])]
                    if n >= 2:
                        siw = [(dpo, pout2_id[n - 2])]
                    sgin = S.emit("sp", lambda h, n=n, hbuf=hbuf: h.dma_start(
                        pstage[:, hbuf, :c.TT * c.PW].rearrange("p (t w) -> p t w", t=c.TT),
                        pscr_r[:, :, n * c.PW:(n + 1) * c.PW]), waits=siw, inc=dpi, n=16)

                    def umm(h, hbuf=hbuf):
                        last = None
                        for t in range(c.TT):
                            for kc in range(c.KD):
                                last = h.matmul(
                                    psA[hbuf][:, t * 512:t * 512 + c.PW],
                                    xtb[:, kc * c.TC + t * 128: kc * c.TC + (t + 1) * 128],
                                    wpan[:, hbuf, kc * c.PW:(kc + 1) * c.PW],
                                    start=(kc == 0), stop=(kc == c.KD - 1))
                        return last
                    uwaits = [(dws, uld)]
                    if n < 2:
                        uwaits += [("act", A1_act), ("dve", A1_dve)]
                    else:
                        uwaits.append(("dve", pm_done[(n - 2, c.TT - 1)]))
                    u_done[n] = S.emit("pe", umm, waits=uwaits, inc="pe")

                    for t in range(c.TT):
                        pm_done[(n, t)] = S.emit("dve", lambda h, n=n, t=t, hbuf=hbuf: h.tensor_tensor(
                            pstage[:, hbuf, t * c.PW:(t + 1) * c.PW],
                            pstage[:, hbuf, t * c.PW:(t + 1) * c.PW],
                            psA[hbuf][:, t * 512:t * 512 + c.PW], op=ALU.mult),
                            waits=[(dpi, sgin), ("pe", u_done[n])], inc="dve")
                    pout2_id[n] = S.emit("sp", lambda h, n=n, hbuf=hbuf: h.dma_start(
                        pscr_r[:, :, n * c.PW:(n + 1) * c.PW],
                        pstage[:, hbuf, :c.TT * c.PW].rearrange("p (t w) -> p t w", t=c.TT)),
                        waits=[("dve", pm_done[(n, c.TT - 1)])], inc=dpo, n=16)

                ncursor = 0
                for k in range(max(c.BI_IT, (c.NPAN + 1) // 2)):
                    if k < c.BI_IT:
                        count_iter(k)
                    for _ in range(2):
                        if ncursor < c.NPAN:
                            a2_panel(ncursor)
                            ncursor += 1
                while ncursor < c.NPAN:
                    a2_panel(ncursor)
                    ncursor += 1
                S.emit("dve", lambda h: h.tensor_copy(tfin[:], lo[:]), inc="dve")
                count_done = S.cnt["dve"]
                count_act = S.cnt["act"]
                phaseA_pe = S.cnt["pe"]
                pout_all = [("dpoa", S.cnt["dpoa"]), ("dpob", S.cnt["dpob"])]

            if stop_after <= 2:
                finish_early()
                nc.finalize()
                return nc
            with nc.psum_tensor(f"trp0{ch}", [128, 4 * c.TT * 128], BF16) as trp0, \
                 nc.psum_tensor(f"trp1{ch}", [128, 4 * c.TT * 128], BF16) as trp1:
                trp = [trp0, trp1]
                NB = (c.ITI + QB - 1) // QB
                evac_done = [0] * NB
                mask_done = [0] * NB
                tr_done_g = [0] * NB
                pin_id = {}

                def pcover(b):
                    it0 = b * QB
                    nits = min(QB, c.ITI - it0)
                    hi = min(c.NPAN - 1, ((it0 + nits) * 128 + c.PW - 1) // c.PW)
                    w = []
                    ev = [p for p in range(0, hi + 1, 2)]
                    od = [p for p in range(1, hi + 1, 2)]
                    if ev:
                        w.append(("dpoa", pout2_id[ev[-1]]))
                    if od:
                        w.append(("dpob", pout2_id[od[-1]]))
                    return w

                def emit_pin(b, pw):
                    it0 = b * QB
                    nits = min(QB, c.ITI - it0)
                    dpi = "dpia" if b % 2 == 0 else "dpib"
                    pin_id[b] = S.emit("sp", lambda h, it0=it0, nits=nits, b=b: h.dma_start(
                        scb2[:, b % 2, :nits * c.TT * 128].rearrange("p (t w) -> p t w", t=c.TT),
                        pscr_r[:, :, it0 * 128:(it0 + nits) * 128]),
                        waits=pw, inc=dpi, n=16)

                for t in range(c.TT):
                    S.emit("dve", lambda h, t=t: h.tensor_copy(
                        tbig[:, t * 128:(t + 1) * 128],
                        tfin[:, t:t + 1].broadcast_to((128, 128))), inc="dve")
                tbig_dve = S.cnt["dve"]
                pinw0 = [("dve", count_done), ("act", count_act)]
                emit_pin(0, pinw0 + pcover(0))
                if NB > 1:
                    emit_pin(1, pinw0 + pcover(1))
                for b in range(NB):
                    it0 = b * QB
                    nits = min(QB, c.ITI - it0)
                    dpi = "dpia" if b % 2 == 0 else "dpib"
                    mw = [(dpi, pin_id[b]), ("dve", tbig_dve)]
                    if b == 0:
                        mw += [("dsa", S.cnt["dsa"]), ("dsb", S.cnt["dsb"])]
                    if b >= 2:
                        mw.append(("pe", tr_done_g[b - 2]))
                    for gi in range(nits):
                        it = it0 + gi
                        av = arena_f16[:].rearrange("p (t i) -> p t i", t=c.TT)[
                            :, :, it * 128:(it + 1) * 128]
                        pv = scb2[:, (it // QB) % 2,
                                  :nits * c.TT * 128].rearrange(
                            "p (t w) -> p t w", t=c.TT)[:, :, gi * 128:(gi + 1) * 128]
                        tb = tbig[:].rearrange("p (t q) -> p t q", t=c.TT)
                        smx = smt2[b % 2]
                        S.emit("dve", lambda h, av=av, tb=tb, smx=smx, gi=gi: h.tensor_tensor(
                            smx[:, gi, :].rearrange("p (t q) -> p t q", t=c.TT),
                            av, tb, op=ALU.is_gt),
                            waits=mw if gi == 0 else (), inc="dve")
                        mask_done[b] = S.emit("dve", lambda h, pv=pv, smx=smx, gi=gi: h.tensor_tensor(
                            smx[:, gi, :].rearrange("p (t q) -> p t q", t=c.TT),
                            smx[:, gi, :].rearrange("p (t q) -> p t q", t=c.TT),
                            pv, op=ALU.mult), inc="dve")
                    def trf(h, b=b, it0=it0, nits=nits):
                        last = None
                        smx = smt2[b % 2]
                        for gi in range(nits):
                            for t in range(c.TT):
                                last = h.transpose(
                                    trp[b % 2][:, gi * c.TT * 128 + t * 128: gi * c.TT * 128 + (t + 1) * 128],
                                    smx[:, gi, t * 128:(t + 1) * 128], ident[:])
                        return last
                    tw = [("dve", mask_done[b]), ("dve", ident_dve)]
                    if b >= 2:
                        tw.append(("act", evac_done[b - 2]))
                    tr_done_g[b] = S.emit("pe", trf, waits=tw, inc="pe")
                    for gi in range(nits):
                        it = it0 + gi
                        evac_done[b] = S.emit("act", lambda h, b=b, it=it, gi=gi: h.copy(
                            smTv[:, :, it * 128:(it + 1) * 128],
                            trp[b % 2][:, gi * c.TT * 128:(gi + 1) * c.TT * 128].rearrange(
                                "p (t q) -> p t q", t=c.TT)),
                            waits=[("pe", tr_done_g[b])] if gi == 0 else (), inc="act")
                    if b + 2 < NB:
                        emit_pin(b + 2, [("dve", mask_done[b])] + pcover(b + 2))
                smT_done = evac_done[NB - 1]

            if stop_after <= 3:
                finish_early()
                nc.finalize()
                return nc
            with nc.psum_tensor(f"dps0{ch}", [128, c.TT * 128], F32) as dps0, \
                 nc.psum_tensor(f"dps1{ch}", [128, c.TT * 128], F32) as dps1, \
                 nc.psum_tensor(f"ssum{ch}", [1, c.TC], F32) as ssum, \
                 nc.psum_tensor(f"ssq{ch}", [1, c.TC], F32) as ssq, \
                 nc.psum_tensor(f"xa1r{ch}", [2, c.TC], F32) as xa1rp, \
                 nc.psum_tensor(f"xa3r{ch}", [2, c.TC], F32) as xa3rp, \
                 nc.psum_tensor(f"mbc{ch}", [128, c.TC], F32) as mbc, \
                 nc.psum_tensor(f"rbc{ch}", [128, c.TC], F32) as rbc:
                dps = [dps0, dps1]
                dn_done = [0] * c.HT
                r_done = [0] * c.HT
                sq_done = [0] * c.HT
                st_done = [0] * c.HT
                wld = [0] * c.HT
                xld = [0] * c.HT
                sp_done = [0] * c.HT
                for ht in range(c.HT):
                    h2 = ht % 2
                    dw = "dwa" if h2 == 0 else "dwb"
                    dsx = "dsa" if h2 == 0 else "dsb"
                    if ht < 2:
                        ww = [("pe", u_done[c.NPAN - 2 + (ht % 2)])]
                    else:
                        ww = [("pe", dn_done[ht - 2])]
                    wld[ht] = S.emit("pool", lambda h, ht=ht, h2=h2: h.dma_start(
                        wpan[:, h2, :c.ITI * 128], wd[ht]), waits=ww, inc=dw, n=16)
                    xw = ([("dve", dve_end_prev_chunk), ("pe", tr_done_g[NB - 1]),
                           ("act", smT_done)] if ht < 2 else [("dve", r_done[ht - 2])])
                    xld[ht] = S.emit("sp", lambda h, ht=ht, h2=h2, t0=t0: h.dma_start(
                        SA[:, h2, :], xT[ht * 128:(ht + 1) * 128, t0:t0 + c.TC]),
                        waits=xw, inc=dsx, n=16)

                    def dmm(h, ht=ht, h2=h2):
                        last = None
                        for it in range(c.ITI):
                            last = h.matmul(
                                dps[h2][:].rearrange("p (t q) -> p t q", t=c.TT),
                                wpan[:, h2, it * 128:(it + 1) * 128],
                                smTv[:, :, it * 128:(it + 1) * 128],
                                start=(it == 0), stop=(it == c.ITI - 1))
                        return last
                    dwts = [(dw, wld[ht]), ("act", smT_done)]
                    if ht >= 2:
                        dwts.append(("dve", r_done[ht - 2]))
                    dn_done[ht] = S.emit("pe", dmm, waits=dwts, inc="pe")

                    dsp = "dspa" if h2 == 0 else "dspb"
                    rw = [("pe", dn_done[ht]), (dsx, xld[ht])]
                    if ht >= 2:
                        rw += [(dsp, sp_done[ht - 2]), ("pe", st_done[ht - 2])]
                    else:
                        rw += [("douta", dout_prev[0]), ("doutb", dout_prev[1])]
                    r_done[ht] = S.emit("dve", lambda h, h2=h2: h.tensor_tensor(
                        SB[:, h2, :], SA[:, h2, :], dps[h2][:], op=ALU.add),
                        waits=rw, inc="dve")
                    j = ht % c.KDF
                    if ht < c.KDF:
                        S.emit("dve", lambda h, j=j, h2=h2: h.tensor_copy(
                            xff_v[:, j * c.TC:(j + 1) * c.TC], SB[:, h2, :]), inc="dve")
                    else:
                        S.emit("dve", lambda h, j=j, h2=h2: h.tensor_tensor(
                            xff_v[:, j * c.TC:(j + 1) * c.TC],
                            xff_v[:, j * c.TC:(j + 1) * c.TC],
                            SB[:, h2, :], op=ALU.add), inc="dve")
                    fold_r_done = S.cnt["dve"]
                    qw = [("dve", r_done[ht])]
                    if ht >= 2:
                        qw.append(("pe", st_done[ht - 2]))
                    sq_done[ht] = S.emit("act", lambda h, h2=h2: h.activation(
                        SC[:, h2, :], SB[:, h2, :], AF.Square), waits=qw, inc="act")
                    def stats_mm(hs):
                        S.emit("pe", lambda h, hs=hs: h.matmul(
                            ssum[:], onec[:], SB[:, hs % 2, :], start=(hs == 0), stop=(hs == c.HT - 1)),
                            waits=[("dve", r_done[hs]), ("dve", ident_dve)], inc="pe")
                        S.emit("pe", lambda h, hs=hs: h.matmul(
                            xa1rp[:], a1sbf[:, hs * 2:(hs + 1) * 2], SB[:, hs % 2, :],
                            start=(hs == 0), stop=(hs == c.HT - 1)), inc="pe")
                        S.emit("pe", lambda h, hs=hs: h.matmul(
                            xa3rp[:], a3sbf[:, hs * 2:(hs + 1) * 2], SB[:, hs % 2, :],
                            start=(hs == 0), stop=(hs == c.HT - 1)), inc="pe")
                        st_done[hs] = S.emit("pe", lambda h, hs=hs: h.matmul(
                            ssq[:], onec[:], SC[:, hs % 2, :], start=(hs == 0), stop=(hs == c.HT - 1)),
                            waits=[("act", sq_done[hs])], inc="pe")
                    if ht >= 1:
                        stats_mm(ht - 1)
                    if ht == c.HT - 1:
                        stats_mm(ht)
                    sp_done[ht] = S.emit("sp", lambda h, ht=ht, h2=h2: h.dma_start(
                        rscr[ht], SB[:, h2, :]), waits=[("dve", r_done[ht])], inc=dsp, n=16)
                stats_pe = st_done[c.HT - 1]
                spill_all = [("dspa", S.cnt["dspa"]), ("dspb", S.cnt["dspb"])]
                S.emit("dve", lambda h: h.tensor_copy(SA[0:2, 0, :], xa1rp[:]),
                       waits=[("pe", stats_pe)], inc="dve")
                S.emit("dve", lambda h: h.tensor_copy(SA[0:2, 1, :], xa3rp[:]), inc="dve")

                S.emit("dve", lambda h: h.tensor_scalar_mul(meanb[:], ssum[:], invD),
                       waits=[("pe", stats_pe)], inc="dve")
                S.emit("dve", lambda h: h.tensor_scalar_mul(stA[:], ssq[:], invD), inc="dve")
                S.emit("dve", lambda h: h.tensor_mul(stB[:], meanb[:], meanb[:]), inc="dve")
                S.emit("dve", lambda h: h.tensor_sub(stA[:], stA[:], stB[:]), inc="dve")
                S.emit("dve", lambda h: h.tensor_scalar_add(stA[:], stA[:], 1e-5), inc="dve")
                var_dve = S.cnt["dve"]
                S.emit("act", lambda h: h.activation(stB[:], stA[:], AF.Sqrt),
                       waits=[("dve", var_dve)], inc="act")
                S.emit("dve", lambda h: h.reciprocal(rstdb[:], stB[:]),
                       waits=[("act", S.cnt["act"])], inc="dve")
                ln1_dve = S.cnt["dve"]

                S.emit("pe", lambda h: h.matmul(mbc[:], oner[:], meanb[:], start=True, stop=True),
                       waits=[("dve", ln1_dve)], inc="pe")
                bc_pe = S.emit("pe", lambda h: h.matmul(rbc[:], oner[:], rstdb[:], start=True, stop=True),
                               inc="pe")
                S.emit("dve", lambda h: h.tensor_copy(mrsb[:, 0, :], mbc[:]),
                       waits=[("pe", bc_pe)], inc="dve")
                bc_cp = S.emit("dve", lambda h: h.tensor_copy(mrsb[:, 1, :], rbc[:]), inc="dve")

                S.emit("dve", lambda h: h.tensor_scalar_mul(SC[:, 0, :], mrsb[:, 0, :], 4.0), inc="dve")
                for j in range(c.KDF):
                    S.emit("dve", lambda h, j=j: h.tensor_sub(
                        SC[:, 1, :], xff_v[:, j * c.TC:(j + 1) * c.TC], SC[:, 0, :]), inc="dve")
                    S.emit("dve", lambda h, j=j: h.tensor_mul(
                        xfbf(j), SC[:, 1, :], mrsb[:, 1, :]),
                        waits=pout_all, inc="dve")
                xf_dve = S.cnt["dve"]
                xa_act = 0
                for (col, dst) in ((0, 0), (1, 1)):
                    sl = 1 - col
                    S.emit("dve", lambda h, col=col, sl=sl: h.tensor_scalar(
                        SC[0:2, sl, :], mrsb[0:2, 0, :], asums[0:2, col:col + 1], None,
                        op0=ALU.mult), inc="dve")
                    S.emit("dve", lambda h, col=col, sl=sl: h.tensor_sub(
                        SC[0:2, sl, :], SA[0:2, col, :], SC[0:2, sl, :]), inc="dve")
                    d_ = S.emit("dve", lambda h, sl=sl: h.tensor_mul(
                        SC[0:2, sl, :], SC[0:2, sl, :], mrsb[0:2, 1, :]), inc="dve")
                    xa_act = S.emit("act", lambda h, dst=dst, sl=sl: h.mul(
                        xa[:, dst * c.TC:(dst + 1) * c.TC], SC[0:2, sl, :], 0.5),
                        waits=[("dve", d_)], inc="act")


            if stop_after <= 5:
                finish_early()
                nc.finalize()
                return nc
            with nc.psum_tensor(f"q1a{ch}", [128, c.TC], F32) as q1a, \
                 nc.psum_tensor(f"q1b{ch}", [128, c.TC], F32) as q1b, \
                 nc.psum_tensor(f"q1c{ch}", [128, c.TC], F32) as q1c, \
                 nc.psum_tensor(f"q3a{ch}", [128, c.TC], F32) as q3a, \
                 nc.psum_tensor(f"q3b{ch}", [128, c.TC], F32) as q3b, \
                 nc.psum_tensor(f"q3c{ch}", [128, c.TC], F32) as q3c, \
                 nc.psum_tensor(f"xa2p{ch}", [2, c.TC], F32) as xa2p:
                q1ps = [q1a, q1b, q1c]
                q3ps = [q3a, q3b, q3c]
                NBQ = (c.ITI + QF - 1) // QF
                wq_ld = [0] * NBQ
                q_done = [0] * c.ITI
                h_done = [0] * c.ITI
                xa2_done = [0] * c.ITI

                def emit_wq(b, ww):
                    it0 = b * QF
                    nits = min(QF, c.ITI - it0)
                    dw = ("dwa", "dwb", "dxt")[b % 3]
                    wq_ld[b] = S.emit("pool", lambda h, it0=it0, nits=nits, b=b: h.dma_start(
                        wpan3[:, b % 3, :nits * W13B].rearrange("p (i f) -> p i f", i=nits),
                        w13[it0:it0 + nits].rearrange("i p f -> p i f")), waits=ww, inc=dw, n=16)

                emit_wq(0, [("pe", stats_pe)])
                if NBQ > 1:
                    emit_wq(1, [("pe", stats_pe)])
                if NBQ > 2:
                    emit_wq(2, [("pe", stats_pe)])
                for it in range(c.ITI):
                    b = it // QF
                    bo = it - b * QF
                    dw = ("dwa", "dwb", "dxt")[b % 3]
                    i2 = it % 3

                    def qmm(h, base, ps, xap, i2=i2, b=b):
                        last = None
                        for kc in range(c.KDF):
                            last = h.matmul(ps[i2][:], wpan3[:, b % 3, base + kc * 128: base + (kc + 1) * 128],
                                            xfbf(kc), start=(kc == 0), stop=False)
                        last = h.matmul(ps[i2][:], wpan3[0:2, b % 3, base + c.KDF * 128: base + (c.KDF + 1) * 128],
                                        xap, start=False, stop=True)
                        return last
                    qw = [(dw, wq_ld[b]), ("act", xa_act), ("dve", xf_dve)]
                    if it >= 3:
                        qw.append(("dve", h_done[it - 3]))
                    S.emit("pe", lambda h, qmm=qmm, bo=bo: qmm(h, bo * W13B, q1ps, xa[:, 0:c.TC]),
                           waits=qw, inc="pe")
                    q_done[it] = S.emit("pe", lambda h, qmm=qmm, bo=bo: qmm(
                        h, bo * W13B + W1B, q3ps, xa[:, c.TC:2 * c.TC]), inc="pe")

                    aw = [("pe", q_done[it])]
                    if it >= 2:
                        aw.append(("dve", h_done[it - 2]))
                    sq1 = S.emit("act", lambda h, i2=i2, s2=it % 2: h.activation(
                        SC[:, s2, :], q1ps[i2][:], AF.Silu), waits=aw, inc="act")
                    hw = [("act", sq1)]
                    if it == 0:
                        hw.append(("pe", stats_pe))
                    if it >= 2:
                        hw.append(("pe", xa2_done[it - 2]))
                    h_done[it] = S.emit("dve", lambda h, it=it, i2=i2, s2=it % 2: h.tensor_tensor(
                        arena[:, it * c.TC:(it + 1) * c.TC], SC[:, s2, :], q3ps[i2][:],
                        op=ALU.mult), waits=hw, inc="dve")
                    def xa2_mm(ix):
                        xa2_done[ix] = S.emit("pe", lambda h, ix=ix: h.matmul(
                            xa2p[:], a2sb[:, ix * 2:(ix + 1) * 2], arena[:, ix * c.TC:(ix + 1) * c.TC],
                            start=(ix == 0), stop=(ix == c.ITI - 1)),
                            waits=[("dve", h_done[ix])], inc="pe")
                    if it >= 1:
                        xa2_mm(it - 1)
                    if it == c.ITI - 1:
                        xa2_mm(it)
                    if bo == min(QF, c.ITI - b * QF) - 1 and b + 3 < NBQ:
                        emit_wq(b + 3, [("pe", q_done[it])])
                S.emit("act", lambda h: h.mul(xa[:, 2 * c.TC:3 * c.TC], xa2p[:], 0.5),
                       waits=[("pe", xa2_done[c.ITI - 1])], inc="act")
                xa2_act = S.cnt["act"]
                hq_dve = S.cnt["dve"]

            if stop_after <= 6:
                finish_early()
                nc.finalize()
                return nc
            by_j = [[] for _ in range(c.KIF)]
            for it in range(c.ITI):
                for (p0, run, j, off) in fold_segments(it, c.IF):
                    by_j[j].append((it, p0, run, off))
            with nc.psum_tensor(f"hfp0{ch}", [128, c.TC], F32) as hfp0, \
                 nc.psum_tensor(f"hfp1{ch}", [128, c.TC], F32) as hfp1:
                hfp = [hfp0, hfp1]
                hfev = [0, 0]
                for j in range(c.KIF):
                    def fmm(h, j=j):
                        last = None
                        for q, (it, p0, run, off) in enumerate(by_j[j]):
                            if run == 128:
                                E = ident
                            elif p0 == 0 and off == 0:
                                E = idlow
                            elif p0 == 0 and off == 64:
                                E = eplus
                            elif p0 == 64 and off == 0:
                                E = eminus
                            else:
                                raise AssertionError((p0, run, off))
                            last = h.matmul(hfp[j % 2][:], E[:],
                                            arena[:, it * c.TC:(it + 1) * c.TC],
                                            start=(q == 0), stop=(q == len(by_j[j]) - 1))
                        return last
                    fw = [("dve", hq_dve), ("act", xa2_act)]
                    if j >= 2:
                        fw.append(("dve", hfev[j % 2]))
                    fpe = S.emit("pe", fmm, waits=fw, inc="pe")
                    hfev[j % 2] = S.emit("dve", lambda h, j=j: h.tensor_copy(
                        arena[:, j * c.TC:(j + 1) * c.TC], hfp[j % 2][:]),
                        waits=[("pe", fpe)], inc="dve")
                hf_dve = S.cnt["dve"]
                fold_pe_last = S.cnt["pe"]

            if stop_after <= 7:
                finish_early()
                nc.finalize()
                return nc
            with nc.psum_tensor(f"o2a{ch}", [128, c.TC], F32) as o2a, \
                 nc.psum_tensor(f"o2b{ch}", [128, c.TC], F32) as o2b, \
                 nc.psum_tensor(f"ssum2{ch}", [1, c.TC], F32) as ssum2, \
                 nc.psum_tensor(f"ssq2{ch}", [1, c.TC], F32) as ssq2, \
                 nc.psum_tensor(f"mbc2{ch}", [128, c.TC], F32) as mbc2, \
                 nc.psum_tensor(f"rbc2{ch}", [128, c.TC], F32) as rbc2:
                o2ps = [o2a, o2b]
                w2ld = [0] * c.HT
                o2_done = [0] * c.HT
                r2_done = [0] * c.HT
                sq2_done = [0] * c.HT
                st2_done = [0] * c.HT
                x2in = [0] * c.HT
                r2sp = [0] * c.HT
                for ht in range(c.HT):
                    h2 = ht % 2
                    dw = "dwa" if h2 == 0 else "dwb"
                    dsx = "dsa" if h2 == 0 else "dsb"
                    ww = [("pe", S.cnt["pe"])] if ht < 2 else [("pe", o2_done[ht - 2])]
                    if ht < 2:
                        ww = [("pe", xa2_done[c.ITI - 1])]
                    w2ld[ht] = S.emit("pool", lambda h, ht=ht, h2=h2: h.dma_start(
                        wpan[:, h2, :(c.KIF + 1) * 128], w2[ht]), waits=ww, inc=dw, n=16)
                    dsp = "dspa" if h2 == 0 else "dspb"
                    xw = list(spill_all) + [("dve", xf_dve), ("act", xa_act)]
                    if ht >= 2:
                        xw = [("dve", r2_done[ht - 2])]
                    x2in[ht] = S.emit("sp", lambda h, ht=ht, h2=h2: h.dma_start(
                        SA[:, h2, :], rscr[ht]), waits=xw, inc=dsx, n=16)

                    def o2mm(h, ht=ht, h2=h2):
                        last = None
                        for kc in range(c.KIF):
                            last = h.matmul(o2ps[h2][:], wpan[:, h2, kc * 128:(kc + 1) * 128],
                                            arena[:, kc * c.TC:(kc + 1) * c.TC],
                                            start=(kc == 0), stop=False)
                        last = h.matmul(o2ps[h2][:], wpan[0:2, h2, c.KIF * 128:(c.KIF + 1) * 128],
                                        xa[:, 2 * c.TC:3 * c.TC], start=False, stop=True)
                        return last
                    ow = [(dw, w2ld[ht]), ("dve", hf_dve), ("act", xa2_act)]
                    if ht >= 2:
                        ow.append(("dve", r2_done[ht - 2]))
                    o2_done[ht] = S.emit("pe", o2mm, waits=ow, inc="pe")

                    rw = [("pe", o2_done[ht]), (dsx, x2in[ht])]
                    if ht >= 2:
                        rw += [(dsp, r2sp[ht - 2]), ("pe", st2_done[ht - 2])]
                    S.emit("dve", lambda h, h2=h2: h.tensor_sub(
                        SC[:, h2, :], SA[:, h2, :], mrsb[:, 0, :]), waits=rw, inc="dve")
                    S.emit("dve", lambda h, h2=h2: h.tensor_mul(
                        SC[:, h2, :], SC[:, h2, :], mrsb[:, 1, :]), inc="dve")
                    r2_done[ht] = S.emit("dve", lambda h, h2=h2: h.tensor_tensor(
                        SB[:, h2, :], SC[:, h2, :], o2ps[h2][:], op=ALU.add),
                        waits=[], inc="dve")
                    qw = [("dve", r2_done[ht])]
                    if ht >= 2:
                        qw.append(("pe", st2_done[ht - 2]))
                    sq2_done[ht] = S.emit("act", lambda h, h2=h2: h.activation(
                        SC[:, h2, :], SB[:, h2, :], AF.Square), waits=qw, inc="act")
                    def stats2_mm(hs):
                        S.emit("pe", lambda h, hs=hs: h.matmul(
                            ssum2[:], onec[:], SB[:, hs % 2, :], start=(hs == 0), stop=(hs == c.HT - 1)),
                            waits=[("dve", r2_done[hs])], inc="pe")
                        st2_done[hs] = S.emit("pe", lambda h, hs=hs: h.matmul(
                            ssq2[:], onec[:], SC[:, hs % 2, :], start=(hs == 0), stop=(hs == c.HT - 1)),
                            waits=[("act", sq2_done[hs])], inc="pe")
                    if ht >= 1:
                        stats2_mm(ht - 1)
                    if ht == c.HT - 1:
                        stats2_mm(ht)
                    r2sp[ht] = S.emit("act", lambda h, ht=ht, h2=h2: h.dma_start(
                        rscr[ht], SB[:, h2, :]), waits=[("dve", r2_done[ht])], inc=dsp, n=16)
                stats2_pe = st2_done[c.HT - 1]
                spill2_all = [("dspa", S.cnt["dspa"]), ("dspb", S.cnt["dspb"])]

                S.emit("dve", lambda h: h.tensor_scalar_mul(meanb[:], ssum2[:], invD),
                       waits=[("pe", stats2_pe)], inc="dve")
                S.emit("dve", lambda h: h.tensor_scalar_mul(stA[:], ssq2[:], invD), inc="dve")
                S.emit("dve", lambda h: h.tensor_mul(stB[:], meanb[:], meanb[:]), inc="dve")
                S.emit("dve", lambda h: h.tensor_sub(stA[:], stA[:], stB[:]), inc="dve")
                S.emit("dve", lambda h: h.tensor_scalar_add(stA[:], stA[:], 1e-5), inc="dve")
                var_dve = S.cnt["dve"]
                S.emit("act", lambda h: h.activation(stB[:], stA[:], AF.Sqrt),
                       waits=[("dve", var_dve)], inc="act")
                S.emit("dve", lambda h: h.reciprocal(rstdb[:], stB[:]),
                       waits=[("act", S.cnt["act"])], inc="dve")
                ln2_dve = S.cnt["dve"]

            if True:
                S.emit("pe", lambda h: h.matmul(mbc2[:], oner[:], meanb[:], start=True, stop=True),
                       waits=[("dve", ln2_dve), ("act", S.cnt["act"])], inc="pe")
                bc2_pe = S.emit("pe", lambda h: h.matmul(rbc2[:], oner[:], rstdb[:], start=True, stop=True),
                                inc="pe")
                S.emit("dve", lambda h: h.tensor_copy(mrsb[:, 0, :], mbc2[:]),
                       waits=[("pe", bc2_pe), ("act", S.cnt["act"])], inc="dve")
                bc2_cp = S.emit("dve", lambda h: h.tensor_copy(mrsb[:, 1, :], rbc2[:]), inc="dve")

                out_done = [0] * c.HT
                outst = [0] * c.HT
                for dt in range(c.HT):
                    h2 = dt % 2
                    dsx = "dsa" if h2 == 0 else "dsb"
                    dou = "douta" if h2 == 0 else "doutb"
                    xw = list(spill2_all)
                    if dt >= 2:
                        xw = [("dve", out_done[dt - 2])]
                    rin2 = S.emit("sp", lambda h, dt=dt, h2=h2: h.dma_start(
                        SA[:, h2, :], rscr[dt]), waits=xw, inc=dsx, n=16)
                    tw = [(dsx, rin2), ("dve", bc2_cp)]
                    if dt >= 2:
                        tw.append((dou, outst[dt - 2]))
                    S.emit("dve", lambda h, h2=h2: h.tensor_sub(SC[:, h2, :], SA[:, h2, :], mrsb[:, 0, :]),
                           waits=tw, inc="dve")
                    out_done[dt] = S.emit("dve", lambda h, h2=h2: h.tensor_mul(
                        SB[:, h2, :], SC[:, h2, :], mrsb[:, 1, :]), inc="dve")
                    outst[dt] = S.emit("act", lambda h, dt=dt, h2=h2, t0=t0: h.dma_start(
                        outT[dt * 128:(dt + 1) * 128, t0:t0 + c.TC], SB[:, h2, :]),
                        waits=[("dve", out_done[dt])], inc=dou, n=16)

            pe_end_prev_chunk = S.cnt["pe"]
            dve_end_prev_chunk = S.cnt["dve"]
            xtb_pe_prev = o2_done[c.HT - 1]
            xtb_dve_prev = xf_dve
            arena_pe_prev = o2_done[c.HT - 1]
            ln2psum_prev = bc2_cp

        S.emit("sp", lambda h: h.nop(),
               waits=[("douta", S.cnt["douta"]), ("doutb", S.cnt["doutb"])])
        S.emit("act", lambda h: h.nop(),
               waits=[("douta", S.cnt["douta"]), ("doutb", S.cnt["doutb"])])
        S.emit("pool", lambda h: h.nop(),
               waits=[("douta", S.cnt["douta"]), ("doutb", S.cnt["doutb"])])

        finish()

    nc.finalize()
    return nc



def host_layout(c, L0_attn_w_gate, L0_attn_w_up, L0_attn_w_down,
                L0_ffn_w1q, L0_ffn_b1, L0_ffn_w3q, L0_ffn_b3,
                L0_ffn_w2q, L0_ffn_b2, L0_ffn_a1, L0_ffn_a3, L0_ffn_a2):
    ac = np.ascontiguousarray
    f = np.float32

    def gu_panels(w):
        A = w.reshape(c.NPAN, c.PW, c.KD, 128)
        return ac(A.transpose(0, 3, 2, 1).reshape(c.NPAN, 128, c.KD * c.PW), dtype=f)

    wg_h = gu_panels(L0_attn_w_gate)
    wu_h = gu_panels(L0_attn_w_up)
    B = L0_attn_w_down.reshape(c.HT, 128, c.ITI, 128)
    wd_h = ac(B.transpose(0, 3, 2, 1).reshape(c.HT, 128, c.ITI * 128), dtype=f)

    def w13(wq, b):
        out = np.zeros((c.ITI, 128, (c.KDF + 1) * 128), dtype=f)
        A = wq.reshape(c.ITI, 128, c.KDF, 128).transpose(0, 3, 2, 1)
        out[:, :, :c.KDF * 128] = A.reshape(c.ITI, 128, c.KDF * 128)
        bt = b.reshape(c.ITI, 128, 2).transpose(0, 2, 1)
        out[:, 0:2, c.KDF * 128:] = bt
        return ac(out)

    w1_h = w13(L0_ffn_w1q, L0_ffn_b1)
    w3_h = w13(L0_ffn_w3q, L0_ffn_b3)
    w13_h = ac(np.concatenate([w1_h, w3_h], axis=2))

    w2p = np.zeros((c.D, c.KIF * 128), dtype=f)
    w2p[:, :c.IF] = L0_ffn_w2q
    out2 = np.zeros((c.HT, 128, (c.KIF + 1) * 128), dtype=f)
    A2w = w2p.reshape(c.HT, 128, c.KIF, 128).transpose(0, 3, 2, 1)
    out2[:, :, :c.KIF * 128] = A2w.reshape(c.HT, 128, c.KIF * 128)
    out2[:, 0:2, c.KIF * 128:] = L0_ffn_b2.reshape(c.HT, 128, 2).transpose(0, 2, 1)
    w2_h = ac(out2)

    a1_h = ac(L0_ffn_a1.T.reshape(c.KD, 128, 2), dtype=f)
    a3_h = ac(L0_ffn_a3.T.reshape(c.KD, 128, 2), dtype=f)
    a2_h = ac(L0_ffn_a2.T.reshape(c.ITI, 128, 2), dtype=f)
    return dict(wg=wg_h, wu=wu_h, wd=wd_h, w13=w13_h, w2=w2_h,
                a1=a1_h, a3=a3_h, a2=a2_h)


_FULL = Cfg(D=4096, I=11008, T=1024, PW=344)
N_CORES = 8


def kernel(L0_attn_x, L0_attn_w_gate, L0_attn_w_up, L0_attn_w_down,
           L0_ffn_w1q, L0_ffn_a1, L0_ffn_b1, L0_ffn_w3q, L0_ffn_a3,
           L0_ffn_b3, L0_ffn_w2q, L0_ffn_a2, L0_ffn_b2):
    c = _FULL
    x = np.asarray(L0_attn_x, dtype=np.float32)
    B_, Sq, D = x.shape
    NT = B_ * Sq
    xt = np.ascontiguousarray(x.reshape(NT, D).T)
    weights = host_layout(
        c,
        np.asarray(L0_attn_w_gate, np.float32), np.asarray(L0_attn_w_up, np.float32),
        np.asarray(L0_attn_w_down, np.float32),
        np.asarray(L0_ffn_w1q, np.float32), np.asarray(L0_ffn_b1, np.float32),
        np.asarray(L0_ffn_w3q, np.float32), np.asarray(L0_ffn_b3, np.float32),
        np.asarray(L0_ffn_w2q, np.float32), np.asarray(L0_ffn_b2, np.float32),
        np.asarray(L0_ffn_a1, np.float32), np.asarray(L0_ffn_a3, np.float32),
        np.asarray(L0_ffn_a2, np.float32))

    nc = build_program(c)
    in_maps = []
    for core in range(N_CORES):
        m = dict(weights)
        m["xT"] = np.ascontiguousarray(xt[:, core * c.T:(core + 1) * c.T])
        in_maps.append(m)
    res = run_bass_kernel_spmd(nc, in_maps, core_ids=list(range(N_CORES)))
    outT = np.concatenate([res.results[i]["outT"] for i in range(N_CORES)], axis=1)
    out = np.ascontiguousarray(outT.T).reshape(B_, Sq, D).astype(np.float32)
    return out


# version 15
# speedup vs baseline: 1.0358x; 1.0358x over previous
import numpy as np
from contextlib import ExitStack

import concourse.bass as bass
import concourse.mybir as mybir
from concourse.bass_utils import run_bass_kernel_spmd

F32 = mybir.dt.float32
BF16 = mybir.dt.bfloat16
FP16 = mybir.dt.float16
I32 = mybir.dt.int32
AF = mybir.ActivationFunctionType
ALU = mybir.AluOpType

ENGINES = ("pe", "act", "dve", "pool", "sp")


class Cfg:
    def __init__(self, D, I, T, PW, bi_iters=13, bi_lo=0.125, bi_hi=0.5):
        self.D = D
        self.I = I
        self.T = T
        self.PW = PW
        self.KD = D // 128
        self.ITI = I // 128
        self.NPAN = I // PW
        self.DF = D // 4
        self.KDF = self.DF // 128
        self.IF = I // 4
        self.KIF = (self.IF + 127) // 128
        self.IF_last = self.IF - (self.KIF - 1) * 128
        self.HT = D // 128
        self.NCH = 2
        self.TC = T // 2
        self.TT = self.TC // 128
        self.KK = I // 2
        self.BI_IT = bi_iters
        self.BI_LO = bi_lo
        self.BI_HI = bi_hi
        self.CSL = 8
        assert D % 512 == 0 and I % 256 == 0 and self.IF % 64 == 0
        assert I % PW == 0 and PW <= 512 and I % self.CSL == 0
        assert T % 256 == 0 and self.DF % 128 == 0


class Sch:

    def __init__(self, nc, es):
        self.nc = nc
        self.ops = {e: [] for e in ENGINES}
        self.sem = {}
        self.cnt = {}
        self.lastwait = {e: {} for e in ENGINES}
        self.es = es

    def mksem(self, *names):
        for n in names:
            self.sem[n] = self.es.enter_context(self.nc.semaphore(n))
            self.cnt[n] = 0

    def emit(self, eng, fn, waits=(), inc=None, n=1):
        if eng in ("act", "dve", "pool"):
            waits = list(waits) + [(eng, self.cnt[eng])]
        w = []
        for (s, v) in waits:
            if v > self.lastwait[eng].get(s, -1):
                self.lastwait[eng][s] = v
                w.append((s, v))
        val = None
        if inc is not None:
            self.cnt[inc] += n
            val = self.cnt[inc]
        self.ops[eng].append((tuple(w), fn, inc, n))
        return val

    def run(self, eng, h):
        for w, fn, inc, n in self.ops[eng]:
            for (s, v) in w:
                h.wait_ge(self.sem[s], v)
            ins = fn(h)
            if inc is not None:
                ins.then_inc(self.sem[inc], n)


def fold_segments(it, IF):
    segs = []
    p = 0
    while p < 128:
        f = (128 * it + p) % IF
        j, off = divmod(f, 128)
        run = min(128 - p, 128 - off, IF - f)
        assert off % 64 == 0 and run % 64 == 0 and p % 64 == 0
        segs.append((p, run, j, off))
        p += run
    return segs


def build_program(c: Cfg):
    nc = bass.Bass("TRN2", target_bir_lowering=False)

    xT = nc.dram_tensor("xT", [c.D, c.T], F32, kind="ExternalInput").ap()
    wg = nc.dram_tensor("wg", [c.NPAN, 128, c.KD * c.PW], F32, kind="ExternalInput").ap()
    wu = nc.dram_tensor("wu", [c.NPAN, 128, c.KD * c.PW], F32, kind="ExternalInput").ap()
    wd = nc.dram_tensor("wd", [c.HT, 128, c.ITI * 128], F32, kind="ExternalInput").ap()
    w13 = nc.dram_tensor("w13", [c.ITI, 128, 2 * (c.KDF + 1) * 128], F32, kind="ExternalInput").ap()
    w2 = nc.dram_tensor("w2", [c.HT, 128, (c.KIF + 1) * 128], F32, kind="ExternalInput").ap()
    a1 = nc.dram_tensor("a1", [c.KD, 128, 2], F32, kind="ExternalInput").ap()
    a3 = nc.dram_tensor("a3", [c.KD, 128, 2], F32, kind="ExternalInput").ap()
    a2 = nc.dram_tensor("a2", [c.ITI, 128, 2], F32, kind="ExternalInput").ap()
    outT = nc.dram_tensor("outT", [c.D, c.T], F32, kind="ExternalOutput").ap()
    pscr = nc.dram_tensor("pscr", [c.TT, 128, c.I], BF16, kind="Internal").ap()
    rscr = nc.dram_tensor("rscr", [c.HT, 128, c.TC], F32, kind="Internal").ap()

    xT_r = xT.rearrange("(k p) t -> p k t", p=128)
    pscr_r = pscr.rearrange("t p i -> p t i")

    CW = c.I // c.CSL
    PSA = max(c.TT * c.PW, CW, c.KDF * c.TC // 2)
    PSW = PSA + 4 * c.TT * 128
    W1B = (c.KDF + 1) * 128
    QB = 4
    W13B = 2 * W1B
    WPW = max(c.KD * c.PW, c.ITI * 128, (c.KIF + 1) * 128)
    W3S = (2 * WPW) // 3
    QF = max(1, W3S // W13B)
    QF = min(3, QF)
    assert QF * W13B <= W3S
    HFW = c.KIF * c.TC
    invD = 1.0 / c.D

    es = ExitStack()
    with es:
        ident_i = es.enter_context(nc.sbuf_tensor("ident_i", [128, 128], I32))
        ident = es.enter_context(nc.sbuf_tensor("ident", [128, 128], BF16))
        eplus = es.enter_context(nc.sbuf_tensor("eplus", [128, 128], BF16))
        eminus = es.enter_context(nc.sbuf_tensor("eminus", [128, 128], BF16))
        idlow = es.enter_context(nc.sbuf_tensor("idlow", [128, 128], BF16))
        xtb = es.enter_context(nc.sbuf_tensor("xtb", [128, c.KD * c.TC], BF16))
        wpan = es.enter_context(nc.sbuf_tensor("wpan", [128, 2, WPW], BF16))
        wpan3 = wpan[:].rearrange("p a b -> p (a b)")[:, :3 * W3S].rearrange("p (s w) -> p s w", s=3, w=W3S)
        arena_h = es.enter_context(nc.sbuf_tensor("arena", [128, c.TT * c.I], BF16))
        pstage = es.enter_context(nc.sbuf_tensor("pstage", [128, 2, PSW], BF16))
        sgt = es.enter_context(nc.sbuf_tensor("sgt", [128, c.TT, c.PW], F32))
        mrsb = es.enter_context(nc.sbuf_tensor("mrsb", [128, 2, c.TC], F32))
        SA = es.enter_context(nc.sbuf_tensor("SA", [128, 2, c.TC], F32))
        SB = es.enter_context(nc.sbuf_tensor("SB", [128, 2, c.TC], F32))
        SC = es.enter_context(nc.sbuf_tensor("SC", [128, 2, c.TC], F32))
        a1sbf = es.enter_context(nc.sbuf_tensor("a1sbf", [128, c.KD * 2], F32))
        a3sbf = es.enter_context(nc.sbuf_tensor("a3sbf", [128, c.KD * 2], F32))
        asums = es.enter_context(nc.sbuf_tensor("asums", [2, 2], F32))
        a2sb = es.enter_context(nc.sbuf_tensor("a2sb", [128, c.ITI * 2], BF16))
        onec = es.enter_context(nc.sbuf_tensor("onec", [128, 1], F32))
        oner = es.enter_context(nc.sbuf_tensor("oner", [1, 128], F32))
        lo = es.enter_context(nc.sbuf_tensor("lo", [128, c.TT], F32))
        hi = es.enter_context(nc.sbuf_tensor("hi", [128, c.TT], F32))
        mid = es.enter_context(nc.sbuf_tensor("mid", [128, c.TT], F32))
        cntb = es.enter_context(nc.sbuf_tensor("cntb", [128, c.TT], F32))
        cpart = es.enter_context(nc.sbuf_tensor("cpart", [128, c.TT * c.CSL], F32))
        predb = es.enter_context(nc.sbuf_tensor("predb", [128, c.TT], F32))
        bist = es.enter_context(nc.sbuf_tensor("bist", [128, c.TT], F32))
        negmid = es.enter_context(nc.sbuf_tensor("negmid", [128, c.TT], F32))
        tfin = es.enter_context(nc.sbuf_tensor("tfin", [128, c.TT], F32))
        tbig = es.enter_context(nc.sbuf_tensor("tbig", [128, c.TT * 128], F32))
        rstdb = es.enter_context(nc.sbuf_tensor("rstdb", [1, c.TC], F32))

        arena = arena_h[:]
        xff_v = xtb.bitcast(F32)[:]
        smt_a = SA.bitcast(BF16)[:].rearrange("p a b -> p (a b)").rearrange(
            "p (g w) -> p g w", g=4)
        smt_b = SB.bitcast(BF16)[:].rearrange("p a b -> p (a b)").rearrange(
            "p (g w) -> p g w", g=4)
        smt2 = (smt_a, smt_b)
        scb2 = pstage[:, :, PSA:]
        xa = pstage.rearrange("p a b -> p (a b)")[0:2, PSA:PSA + 3 * c.TC]
        meanb = tbig[0:1, :]
        SCr0 = SC.rearrange("p a b -> p (a b)")
        stA = SCr0[0:1, 0:c.TC]
        stB = SCr0[0:1, c.TC:2 * c.TC]
        arena_f16 = arena_h.bitcast(FP16)
        arena_f32 = arena_h.bitcast(F32)
        smTv = arena_h[:].rearrange("p (t i) -> p t i", t=c.TT)

        def xfbf(j):
            half = (j * c.TC) // PSA
            off = (j * c.TC) % PSA
            return pstage[:, half, off:off + c.TC]

        S = Sch(nc, es)
        S.mksem("pe", "act", "dve", "pool")
        S.mksem("dxt", "dwa", "dwb", "dwa2", "dwb2", "dpoa", "dpob", "dpia", "dpib",
                "dsa", "dsb", "dspa", "dspb", "douta", "doutb", "dmisc")
        SEM = S.sem
        stop_after = getattr(c, "stop_after", 99)

        def finish():
            with nc.Block() as blk:
                @blk.tensor
                def _(h):
                    S.run("pe", h)

                @blk.scalar
                def _(h):
                    S.run("act", h)

                @blk.vector
                def _(h):
                    S.run("dve", h)

                @blk.gpsimd
                def _(h):
                    S.run("pool", h)

                @blk.sync
                def _(h):
                    S.run("sp", h)

        def finish_early():
            mz = S.emit("dve", lambda h: h.memset(SB[:, 0, :], 0.0), inc="dve")
            st = S.emit("sp", lambda h: h.dma_start(outT[0:128, 0:c.TC], SB[:, 0, :]),
                        waits=[("dve", mz)], inc="douta", n=16)
            S.emit("sp", lambda h: h.nop(), waits=[("douta", st)])
            finish()

        S.emit("pool", lambda h: h.iota(ident_i[:], [[1, 128]], base=0, channel_multiplier=-1), inc="pool")
        S.emit("dve", lambda h: h.tensor_scalar(ident[:], ident_i[:], 0.0, None, op0=ALU.is_equal),
               waits=[("pool", S.cnt["pool"])], inc="dve")
        S.emit("pool", lambda h: h.dma_start(
            a1sbf[:].rearrange("p (k m) -> p k m", m=2), a1.rearrange("k p m -> p k m")), inc="dmisc", n=16)
        S.emit("pool", lambda h: h.dma_start(
            a3sbf[:].rearrange("p (k m) -> p k m", m=2), a3.rearrange("k p m -> p k m")), inc="dmisc", n=16)
        S.emit("pool", lambda h: h.dma_start(
            a2sb[:].rearrange("p (k m) -> p k m", m=2), a2.rearrange("k p m -> p k m")), inc="dmisc", n=16)
        S.emit("pool", lambda h: h.iota(ident_i[:], [[1, 128]], base=64, channel_multiplier=-1),
               waits=[("dve", S.cnt["dve"])], inc="pool")
        S.emit("dve", lambda h: h.tensor_scalar(eminus[:], ident_i[:], 0.0, None, op0=ALU.is_equal),
               waits=[("pool", S.cnt["pool"])], inc="dve")
        S.emit("pool", lambda h: h.iota(ident_i[:], [[1, 128]], base=-64, channel_multiplier=-1),
               waits=[("dve", S.cnt["dve"])], inc="pool")
        S.emit("dve", lambda h: h.tensor_scalar(eplus[:], ident_i[:], 0.0, None, op0=ALU.is_equal),
               waits=[("pool", S.cnt["pool"])], inc="dve")
        S.emit("dve", lambda h: h.tensor_copy(idlow[0:64, :], ident[0:64, :]), inc="dve")
        S.emit("dve", lambda h: h.memset(idlow[64:128, :], 0.0), inc="dve")
        S.emit("dve", lambda h: h.memset(eplus[64:128, :], 0.0), inc="dve")
        S.emit("dve", lambda h: h.memset(eminus[0:64, :], 0.0), inc="dve")
        S.emit("dve", lambda h: h.memset(onec[:], 1.0), inc="dve")
        S.emit("dve", lambda h: h.memset(oner[:], 1.0), inc="dve")
        misc_done = S.cnt["dmisc"]
        ident_dve = S.cnt["dve"]
        with nc.psum_tensor("asump", [2, 2], F32) as asump:
            def asmm(h):
                last = None
                for kc in range(c.KD):
                    last = h.matmul(asump[:, 0:1], a1sbf[:, kc * 2:(kc + 1) * 2], onec[:],
                                    start=(kc == 0), stop=(kc == c.KD - 1))
                for kc in range(c.KD):
                    last = h.matmul(asump[:, 1:2], a3sbf[:, kc * 2:(kc + 1) * 2], onec[:],
                                    start=(kc == 0), stop=(kc == c.KD - 1))
                return last
            as_pe = S.emit("pe", asmm,
                           waits=[("dmisc", misc_done), ("dve", ident_dve)], inc="pe")
            S.emit("dve", lambda h: h.tensor_copy(asums[:], asump[:]),
                   waits=[("pe", as_pe)], inc="dve")
        prologue_dve = S.cnt["dve"]

        pe_end_prev_chunk = 0
        dve_end_prev_chunk = 0
        xtb_pe_prev = 0
        ln2psum_prev = 0
        xtb_dve_prev = 0
        arena_pe_prev = 0

        for ch in range(c.NCH):
            t0 = ch * c.TC
            dout_prev = (S.cnt["douta"], S.cnt["doutb"])

            xtb_war = [("dve", prologue_dve)]
            if ch > 0:
                xtb_war = [("dve", xtb_dve_prev)]
            xt_ld = S.emit("pool", lambda h, t0=t0: h.dma_start(
                xtb[:].rearrange("p (k t) -> p k t", k=c.KD),
                xT_r[:, :, t0:t0 + c.TC]), waits=xtb_war, inc="dxt", n=16)

            with nc.psum_tensor(f"gps{ch}", [128, c.TT * 512], F32) as gps, \
                 nc.psum_tensor(f"ups{ch}", [128, c.TT * 512], F32) as ups:
                psA = [gps, ups]
                g_done = [0] * c.NPAN
                u_done = [0] * c.NPAN
                silu_done = {}
                sgc_done = {}
                pm_done = {}
                pout_id = [0] * c.NPAN
                pout2_id = [0] * c.NPAN
                arena_war = [("pe", arena_pe_prev)] if ch > 0 else []

                for n in range(c.NPAN):
                    hbuf = n % 2
                    dws = "dwa" if hbuf == 0 else "dwb"
                    dpo = "dpoa" if hbuf == 0 else "dpob"
                    gw = ([("pe", g_done[n - 2])] if n >= 2 else
                          [("pe", xtb_pe_prev), ("dve", ln2psum_prev)])
                    KH = c.KD // 2
                    gld = S.emit("pool", lambda h, n=n, hbuf=hbuf, KH=KH: h.dma_start(
                        wpan[:, hbuf, :KH * c.PW], wg[n][:, :KH * c.PW]),
                        waits=gw, inc=dws, n=16)
                    gld2 = S.emit("pool", lambda h, n=n, hbuf=hbuf, KH=KH: h.dma_start(
                        wpan[:, hbuf, KH * c.PW:c.KD * c.PW], wg[n][:, KH * c.PW:]),
                        waits=gw, inc=dws + "2", n=16)

                    def gmm(h, hbuf=hbuf, k0=0, k1=KH):
                        last = None
                        for t in range(c.TT):
                            for kc in range(k0, k1):
                                last = h.matmul(
                                    psA[hbuf][:, t * 512:t * 512 + c.PW],
                                    xtb[:, kc * c.TC + t * 128: kc * c.TC + (t + 1) * 128],
                                    wpan[:, hbuf, kc * c.PW:(kc + 1) * c.PW],
                                    start=(kc == 0), stop=(kc == c.KD - 1))
                        return last
                    gwaits = [(dws, gld), ("dxt", xt_ld)]
                    if n >= 2:
                        gwaits.append(("act", silu_done[(n - 2, c.TT - 1)]))
                    S.emit("pe", gmm, waits=gwaits, inc="pe")
                    g_done[n] = S.emit(
                        "pe", lambda h, gmm=gmm, hbuf=hbuf, KH=KH: gmm(h, hbuf, KH, c.KD),
                        waits=[(dws + "2", gld2)], inc="pe")

                    for t in range(c.TT):
                        sw = [("pe", g_done[n])]
                        if n >= 1:
                            sw.append(("dve", sgc_done.get((n - 1, t), 0)))
                        silu_done[(n, t)] = S.emit(
                            "act", lambda h, t=t, hbuf=hbuf: h.activation(
                                sgt[:, t, :], psA[hbuf][:, t * 512:t * 512 + c.PW], AF.Silu),
                            waits=sw, inc="act")
                        S.emit("act", lambda h, n=n, t=t: h.activation(
                            arena_f16[:, t * c.I + n * c.PW: t * c.I + (n + 1) * c.PW],
                            sgt[:, t, :], AF.Abs),
                            waits=arena_war if n == 0 else (), inc="act")
                        cw_ = [("act", silu_done[(n, t)])]
                        if n >= 2:
                            cw_.append((dpo, pout_id[n - 2]))
                        sgc_done[(n, t)] = S.emit("dve", lambda h, n=n, t=t: h.tensor_copy(
                            pstage[:, n % 2, t * c.PW:(t + 1) * c.PW], sgt[:, t, :]),
                            waits=cw_, inc="dve")
                    pout_id[n] = S.emit("sp", lambda h, n=n: h.dma_start(
                        pscr_r[:, :, n * c.PW:(n + 1) * c.PW],
                        pstage[:, n % 2, :c.TT * c.PW].rearrange("p (t w) -> p t w", t=c.TT)),
                        waits=[("dve", sgc_done[(n, c.TT - 1)])], inc=dpo, n=16)

                A1_act = S.cnt["act"]
                A1_dve = S.cnt["dve"]
                phaseA_pe = None

                SBb = SB.bitcast(BF16)[:].rearrange("p a b -> p (a b)")
                SCb = SC.bitcast(BF16)[:].rearrange("p a b -> p (a b)")
                cpv = cpart[:].rearrange("p (t s) -> p s t", s=c.CSL)
                NDV = c.CSL // 2
                EPS = 2.0 ** -20
                jw = [("douta", dout_prev[0]), ("doutb", dout_prev[1])]
                S.emit("dve", lambda h: h.memset(lo[:], c.BI_LO), waits=jw, inc="dve")
                S.emit("dve", lambda h: h.memset(hi[:], c.BI_HI),
                       waits=[("act", A1_act)], inc="dve")

                def count_iter(it_bi):
                    S.emit("dve", lambda h: h.tensor_add(mid[:], lo[:], hi[:]), inc="dve")
                    S.emit("dve", lambda h: h.tensor_scalar_mul(mid[:], mid[:], 0.5), inc="dve")
                    nm_dve = S.emit("dve", lambda h: h.tensor_scalar(
                        negmid[:], mid[:], -1.0, -EPS, op0=ALU.mult, op1=ALU.add), inc="dve")
                    act_cnt = None
                    for t in range(c.TT):
                        for sl in range(c.CSL):
                            if sl < NDV:
                                S.emit("dve", lambda h, t=t, sl=sl: h.tensor_scalar(
                                    SBb[:, 0:CW],
                                    arena_f16[:, t * c.I + sl * CW: t * c.I + (sl + 1) * CW],
                                    mid[:, t:t + 1], None, op0=ALU.is_gt, op1=ALU.add,
                                    accum_out=cpart[:, t * c.CSL + sl: t * c.CSL + sl + 1]),
                                    inc="dve")
                            else:
                                act_cnt = S.emit("act", lambda h, t=t, sl=sl: h.activation(
                                    SCb[:, 0:CW],
                                    arena_f16[:, t * c.I + sl * CW: t * c.I + (sl + 1) * CW],
                                    AF.Sign, bias=negmid[:, t:t + 1], scale=1.0,
                                    accum_out=cpart[:, t * c.CSL + sl: t * c.CSL + sl + 1]),
                                    waits=[("dve", nm_dve)] + jw, inc="act")
                    S.emit("dve", lambda h: h.tensor_add(cntb[:], cpv[:, NDV, :], cpv[:, NDV + 1, :]),
                           waits=[("act", act_cnt)], inc="dve")
                    for sl in range(NDV + 2, c.CSL):
                        S.emit("dve", lambda h, sl=sl: h.tensor_add(cntb[:], cntb[:], cpv[:, sl, :]), inc="dve")
                    S.emit("dve", lambda h: h.tensor_scalar(
                        cntb[:], cntb[:], float((c.CSL - NDV) * CW), 0.5,
                        op0=ALU.add, op1=ALU.mult), inc="dve")
                    for sl in range(NDV):
                        S.emit("dve", lambda h, sl=sl: h.tensor_add(cntb[:], cntb[:], cpv[:, sl, :]), inc="dve")
                    S.emit("dve", lambda h: h.tensor_scalar(
                        predb[:], cntb[:], float(c.KK), None, op0=ALU.is_ge), inc="dve")
                    S.emit("dve", lambda h: h.tensor_sub(bist[:], mid[:], lo[:]), inc="dve")
                    S.emit("dve", lambda h: h.tensor_mul(bist[:], bist[:], predb[:]), inc="dve")
                    S.emit("dve", lambda h: h.tensor_add(lo[:], lo[:], bist[:]), inc="dve")
                    S.emit("dve", lambda h: h.tensor_sub(bist[:], hi[:], mid[:]), inc="dve")
                    S.emit("dve", lambda h: h.tensor_mul(bist[:], bist[:], predb[:]), inc="dve")
                    S.emit("dve", lambda h: h.tensor_add(hi[:], mid[:], bist[:]), inc="dve")

                def a2_panel(n):
                    hbuf = n % 2
                    dws = "dwa" if hbuf == 0 else "dwb"
                    dpo = "dpoa" if hbuf == 0 else "dpob"
                    dpi = "dpia" if hbuf == 0 else "dpib"
                    uw = [("pe", u_done[n - 2])] if n >= 2 else [("pe", g_done[c.NPAN - 1])]
                    KH = c.KD // 2
                    uld = S.emit("pool", lambda h, n=n, hbuf=hbuf, KH=KH: h.dma_start(
                        wpan[:, hbuf, :KH * c.PW], wu[n][:, :KH * c.PW]),
                        waits=uw, inc=dws, n=16)
                    uld2 = S.emit("pool", lambda h, n=n, hbuf=hbuf, KH=KH: h.dma_start(
                        wpan[:, hbuf, KH * c.PW:c.KD * c.PW], wu[n][:, KH * c.PW:]),
                        waits=uw, inc=dws + "2", n=16)
                    siw = [(dpo, pout_id[c.NPAN - 2 + hbuf])]
                    if n >= 2:
                        siw = [(dpo, pout2_id[n - 2])]
                    sgin = S.emit("sp", lambda h, n=n, hbuf=hbuf: h.dma_start(
                        pstage[:, hbuf, :c.TT * c.PW].rearrange("p (t w) -> p t w", t=c.TT),
                        pscr_r[:, :, n * c.PW:(n + 1) * c.PW]), waits=siw, inc=dpi, n=16)

                    def umm(h, hbuf=hbuf, k0=0, k1=KH):
                        last = None
                        for t in range(c.TT):
                            for kc in range(k0, k1):
                                last = h.matmul(
                                    psA[hbuf][:, t * 512:t * 512 + c.PW],
                                    xtb[:, kc * c.TC + t * 128: kc * c.TC + (t + 1) * 128],
                                    wpan[:, hbuf, kc * c.PW:(kc + 1) * c.PW],
                                    start=(kc == 0), stop=(kc == c.KD - 1))
                        return last
                    uwaits = [(dws, uld)]
                    if n < 2:
                        uwaits += [("act", A1_act), ("dve", A1_dve)]
                    else:
                        uwaits.append(("dve", pm_done[(n - 2, c.TT - 1)]))
                    S.emit("pe", umm, waits=uwaits, inc="pe")
                    u_done[n] = S.emit(
                        "pe", lambda h, umm=umm, hbuf=hbuf, KH=KH: umm(h, hbuf, KH, c.KD),
                        waits=[(dws + "2", uld2)], inc="pe")

                    for t in range(c.TT):
                        pm_done[(n, t)] = S.emit("dve", lambda h, n=n, t=t, hbuf=hbuf: h.tensor_tensor(
                            pstage[:, hbuf, t * c.PW:(t + 1) * c.PW],
                            pstage[:, hbuf, t * c.PW:(t + 1) * c.PW],
                            psA[hbuf][:, t * 512:t * 512 + c.PW], op=ALU.mult),
                            waits=[(dpi, sgin), ("pe", u_done[n])], inc="dve")
                    pout2_id[n] = S.emit("sp", lambda h, n=n, hbuf=hbuf: h.dma_start(
                        pscr_r[:, :, n * c.PW:(n + 1) * c.PW],
                        pstage[:, hbuf, :c.TT * c.PW].rearrange("p (t w) -> p t w", t=c.TT)),
                        waits=[("dve", pm_done[(n, c.TT - 1)])], inc=dpo, n=16)

                ncursor = 0
                for k in range(max(c.BI_IT, (c.NPAN + 1) // 2)):
                    if k < c.BI_IT:
                        count_iter(k)
                    for _ in range(2):
                        if ncursor < c.NPAN:
                            a2_panel(ncursor)
                            ncursor += 1
                while ncursor < c.NPAN:
                    a2_panel(ncursor)
                    ncursor += 1
                S.emit("dve", lambda h: h.tensor_copy(tfin[:], lo[:]), inc="dve")
                count_done = S.cnt["dve"]
                count_act = S.cnt["act"]
                phaseA_pe = S.cnt["pe"]
                pout_all = [("dpoa", S.cnt["dpoa"]), ("dpob", S.cnt["dpob"])]

            if stop_after <= 2:
                finish_early()
                nc.finalize()
                return nc
            with nc.psum_tensor(f"trp0{ch}", [128, 4 * c.TT * 128], BF16) as trp0, \
                 nc.psum_tensor(f"trp1{ch}", [128, 4 * c.TT * 128], BF16) as trp1:
                trp = [trp0, trp1]
                NB = (c.ITI + QB - 1) // QB
                evac_done = [0] * NB
                mask_done = [0] * NB
                tr_done_g = [0] * NB
                pin_id = {}

                def pcover(b):
                    it0 = b * QB
                    nits = min(QB, c.ITI - it0)
                    hi = min(c.NPAN - 1, ((it0 + nits) * 128 + c.PW - 1) // c.PW)
                    w = []
                    ev = [p for p in range(0, hi + 1, 2)]
                    od = [p for p in range(1, hi + 1, 2)]
                    if ev:
                        w.append(("dpoa", pout2_id[ev[-1]]))
                    if od:
                        w.append(("dpob", pout2_id[od[-1]]))
                    return w

                def emit_pin(b, pw):
                    it0 = b * QB
                    nits = min(QB, c.ITI - it0)
                    dpi = "dpia" if b % 2 == 0 else "dpib"
                    pin_id[b] = S.emit("sp", lambda h, it0=it0, nits=nits, b=b: h.dma_start(
                        scb2[:, b % 2, :nits * c.TT * 128].rearrange("p (t w) -> p t w", t=c.TT),
                        pscr_r[:, :, it0 * 128:(it0 + nits) * 128]),
                        waits=pw, inc=dpi, n=16)

                for t in range(c.TT):
                    S.emit("dve", lambda h, t=t: h.tensor_copy(
                        tbig[:, t * 128:(t + 1) * 128],
                        tfin[:, t:t + 1].broadcast_to((128, 128))), inc="dve")
                tbig_dve = S.cnt["dve"]
                pinw0 = [("dve", count_done), ("act", count_act)]
                emit_pin(0, pinw0 + pcover(0))
                if NB > 1:
                    emit_pin(1, pinw0 + pcover(1))
                for b in range(NB):
                    it0 = b * QB
                    nits = min(QB, c.ITI - it0)
                    dpi = "dpia" if b % 2 == 0 else "dpib"
                    mw = [(dpi, pin_id[b]), ("dve", tbig_dve)]
                    if b == 0:
                        mw += [("dsa", S.cnt["dsa"]), ("dsb", S.cnt["dsb"])]
                    if b >= 2:
                        mw.append(("pe", tr_done_g[b - 2]))
                    for gi in range(nits):
                        it = it0 + gi
                        av = arena_f16[:].rearrange("p (t i) -> p t i", t=c.TT)[
                            :, :, it * 128:(it + 1) * 128]
                        pv = scb2[:, (it // QB) % 2,
                                  :nits * c.TT * 128].rearrange(
                            "p (t w) -> p t w", t=c.TT)[:, :, gi * 128:(gi + 1) * 128]
                        tb = tbig[:].rearrange("p (t q) -> p t q", t=c.TT)
                        smx = smt2[b % 2]
                        S.emit("dve", lambda h, av=av, tb=tb, smx=smx, gi=gi: h.tensor_tensor(
                            smx[:, gi, :].rearrange("p (t q) -> p t q", t=c.TT),
                            av, tb, op=ALU.is_gt),
                            waits=mw if gi == 0 else (), inc="dve")
                        mask_done[b] = S.emit("dve", lambda h, pv=pv, smx=smx, gi=gi: h.tensor_tensor(
                            smx[:, gi, :].rearrange("p (t q) -> p t q", t=c.TT),
                            smx[:, gi, :].rearrange("p (t q) -> p t q", t=c.TT),
                            pv, op=ALU.mult), inc="dve")
                    def trf(h, b=b, it0=it0, nits=nits):
                        last = None
                        smx = smt2[b % 2]
                        for gi in range(nits):
                            for t in range(c.TT):
                                last = h.transpose(
                                    trp[b % 2][:, gi * c.TT * 128 + t * 128: gi * c.TT * 128 + (t + 1) * 128],
                                    smx[:, gi, t * 128:(t + 1) * 128], ident[:])
                        return last
                    tw = [("dve", mask_done[b]), ("dve", ident_dve)]
                    if b >= 2:
                        tw.append(("act", evac_done[b - 2]))
                    tr_done_g[b] = S.emit("pe", trf, waits=tw, inc="pe")
                    for gi in range(nits):
                        it = it0 + gi
                        evac_done[b] = S.emit("act", lambda h, b=b, it=it, gi=gi: h.copy(
                            smTv[:, :, it * 128:(it + 1) * 128],
                            trp[b % 2][:, gi * c.TT * 128:(gi + 1) * c.TT * 128].rearrange(
                                "p (t q) -> p t q", t=c.TT)),
                            waits=[("pe", tr_done_g[b])] if gi == 0 else (), inc="act")
                    if b + 2 < NB:
                        emit_pin(b + 2, [("dve", mask_done[b])] + pcover(b + 2))
                smT_done = evac_done[NB - 1]

            if stop_after <= 3:
                finish_early()
                nc.finalize()
                return nc
            with nc.psum_tensor(f"dps0{ch}", [128, c.TT * 128], F32) as dps0, \
                 nc.psum_tensor(f"dps1{ch}", [128, c.TT * 128], F32) as dps1, \
                 nc.psum_tensor(f"ssum{ch}", [1, c.TC], F32) as ssum, \
                 nc.psum_tensor(f"ssq{ch}", [1, c.TC], F32) as ssq, \
                 nc.psum_tensor(f"xa1r{ch}", [2, c.TC], F32) as xa1rp, \
                 nc.psum_tensor(f"xa3r{ch}", [2, c.TC], F32) as xa3rp, \
                 nc.psum_tensor(f"mbc{ch}", [128, c.TC], F32) as mbc, \
                 nc.psum_tensor(f"rbc{ch}", [128, c.TC], F32) as rbc:
                dps = [dps0, dps1]
                dn_done = [0] * c.HT
                r_done = [0] * c.HT
                sq_done = [0] * c.HT
                st_done = [0] * c.HT
                wld = [0] * c.HT
                xld = [0] * c.HT
                sp_done = [0] * c.HT
                for ht in range(c.HT):
                    h2 = ht % 2
                    dw = "dwa" if h2 == 0 else "dwb"
                    dsx = "dsa" if h2 == 0 else "dsb"
                    if ht < 2:
                        ww = [("pe", u_done[c.NPAN - 2 + (ht % 2)])]
                    else:
                        ww = [("pe", dn_done[ht - 2])]
                    wld[ht] = S.emit("pool", lambda h, ht=ht, h2=h2: h.dma_start(
                        wpan[:, h2, :c.ITI * 128], wd[ht]), waits=ww, inc=dw, n=16)
                    xw = ([("dve", dve_end_prev_chunk), ("pe", tr_done_g[NB - 1]),
                           ("act", smT_done)] if ht < 2 else [("dve", r_done[ht - 2])])
                    xld[ht] = S.emit("sp", lambda h, ht=ht, h2=h2, t0=t0: h.dma_start(
                        SA[:, h2, :], xT[ht * 128:(ht + 1) * 128, t0:t0 + c.TC]),
                        waits=xw, inc=dsx, n=16)

                    def dmm(h, ht=ht, h2=h2):
                        last = None
                        for it in range(c.ITI):
                            last = h.matmul(
                                dps[h2][:].rearrange("p (t q) -> p t q", t=c.TT),
                                wpan[:, h2, it * 128:(it + 1) * 128],
                                smTv[:, :, it * 128:(it + 1) * 128],
                                start=(it == 0), stop=(it == c.ITI - 1))
                        return last
                    dwts = [(dw, wld[ht]), ("act", smT_done)]
                    if ht >= 2:
                        dwts.append(("dve", r_done[ht - 2]))
                    dn_done[ht] = S.emit("pe", dmm, waits=dwts, inc="pe")

                    dsp = "dspa" if h2 == 0 else "dspb"
                    rw = [("pe", dn_done[ht]), (dsx, xld[ht])]
                    if ht >= 2:
                        rw += [(dsp, sp_done[ht - 2]), ("pe", st_done[ht - 2])]
                    else:
                        rw += [("douta", dout_prev[0]), ("doutb", dout_prev[1])]
                    r_done[ht] = S.emit("dve", lambda h, h2=h2: h.tensor_tensor(
                        SB[:, h2, :], SA[:, h2, :], dps[h2][:], op=ALU.add),
                        waits=rw, inc="dve")
                    j = ht % c.KDF
                    if ht < c.KDF:
                        S.emit("dve", lambda h, j=j, h2=h2: h.tensor_copy(
                            xff_v[:, j * c.TC:(j + 1) * c.TC], SB[:, h2, :]), inc="dve")
                    else:
                        S.emit("dve", lambda h, j=j, h2=h2: h.tensor_tensor(
                            xff_v[:, j * c.TC:(j + 1) * c.TC],
                            xff_v[:, j * c.TC:(j + 1) * c.TC],
                            SB[:, h2, :], op=ALU.add), inc="dve")
                    fold_r_done = S.cnt["dve"]
                    qw = [("dve", r_done[ht])]
                    if ht >= 2:
                        qw.append(("pe", st_done[ht - 2]))
                    sq_done[ht] = S.emit("act", lambda h, h2=h2: h.activation(
                        SC[:, h2, :], SB[:, h2, :], AF.Square), waits=qw, inc="act")
                    def stats_mm(hs):
                        S.emit("pe", lambda h, hs=hs: h.matmul(
                            ssum[:], onec[:], SB[:, hs % 2, :], start=(hs == 0), stop=(hs == c.HT - 1)),
                            waits=[("dve", r_done[hs]), ("dve", ident_dve)], inc="pe")
                        S.emit("pe", lambda h, hs=hs: h.matmul(
                            xa1rp[:], a1sbf[:, hs * 2:(hs + 1) * 2], SB[:, hs % 2, :],
                            start=(hs == 0), stop=(hs == c.HT - 1)), inc="pe")
                        S.emit("pe", lambda h, hs=hs: h.matmul(
                            xa3rp[:], a3sbf[:, hs * 2:(hs + 1) * 2], SB[:, hs % 2, :],
                            start=(hs == 0), stop=(hs == c.HT - 1)), inc="pe")
                        st_done[hs] = S.emit("pe", lambda h, hs=hs: h.matmul(
                            ssq[:], onec[:], SC[:, hs % 2, :], start=(hs == 0), stop=(hs == c.HT - 1)),
                            waits=[("act", sq_done[hs])], inc="pe")
                    if ht >= 1:
                        stats_mm(ht - 1)
                    if ht == c.HT - 1:
                        stats_mm(ht)
                    sp_done[ht] = S.emit("sp", lambda h, ht=ht, h2=h2: h.dma_start(
                        rscr[ht], SB[:, h2, :]), waits=[("dve", r_done[ht])], inc=dsp, n=16)
                stats_pe = st_done[c.HT - 1]
                spill_all = [("dspa", S.cnt["dspa"]), ("dspb", S.cnt["dspb"])]
                S.emit("dve", lambda h: h.tensor_copy(SA[0:2, 0, :], xa1rp[:]),
                       waits=[("pe", stats_pe)], inc="dve")
                S.emit("dve", lambda h: h.tensor_copy(SA[0:2, 1, :], xa3rp[:]), inc="dve")

                S.emit("dve", lambda h: h.tensor_scalar_mul(meanb[:], ssum[:], invD),
                       waits=[("pe", stats_pe)], inc="dve")
                S.emit("dve", lambda h: h.tensor_scalar_mul(stA[:], ssq[:], invD), inc="dve")
                S.emit("dve", lambda h: h.tensor_mul(stB[:], meanb[:], meanb[:]), inc="dve")
                S.emit("dve", lambda h: h.tensor_sub(stA[:], stA[:], stB[:]), inc="dve")
                S.emit("dve", lambda h: h.tensor_scalar_add(stA[:], stA[:], 1e-5), inc="dve")
                var_dve = S.cnt["dve"]
                S.emit("act", lambda h: h.activation(stB[:], stA[:], AF.Sqrt),
                       waits=[("dve", var_dve)], inc="act")
                S.emit("dve", lambda h: h.reciprocal(rstdb[:], stB[:]),
                       waits=[("act", S.cnt["act"])], inc="dve")
                ln1_dve = S.cnt["dve"]

                S.emit("pe", lambda h: h.matmul(mbc[:], oner[:], meanb[:], start=True, stop=True),
                       waits=[("dve", ln1_dve)], inc="pe")
                bc_pe = S.emit("pe", lambda h: h.matmul(rbc[:], oner[:], rstdb[:], start=True, stop=True),
                               inc="pe")
                S.emit("dve", lambda h: h.tensor_copy(mrsb[:, 0, :], mbc[:]),
                       waits=[("pe", bc_pe)], inc="dve")
                bc_cp = S.emit("dve", lambda h: h.tensor_copy(mrsb[:, 1, :], rbc[:]), inc="dve")

                S.emit("dve", lambda h: h.tensor_scalar_mul(SC[:, 0, :], mrsb[:, 0, :], 4.0), inc="dve")
                for j in range(c.KDF):
                    S.emit("dve", lambda h, j=j: h.tensor_sub(
                        SC[:, 1, :], xff_v[:, j * c.TC:(j + 1) * c.TC], SC[:, 0, :]), inc="dve")
                    S.emit("dve", lambda h, j=j: h.tensor_mul(
                        xfbf(j), SC[:, 1, :], mrsb[:, 1, :]),
                        waits=pout_all, inc="dve")
                xf_dve = S.cnt["dve"]
                xa_act = 0
                for (col, dst) in ((0, 0), (1, 1)):
                    sl = 1 - col
                    S.emit("dve", lambda h, col=col, sl=sl: h.tensor_scalar(
                        SC[0:2, sl, :], mrsb[0:2, 0, :], asums[0:2, col:col + 1], None,
                        op0=ALU.mult), inc="dve")
                    S.emit("dve", lambda h, col=col, sl=sl: h.tensor_sub(
                        SC[0:2, sl, :], SA[0:2, col, :], SC[0:2, sl, :]), inc="dve")
                    d_ = S.emit("dve", lambda h, sl=sl: h.tensor_mul(
                        SC[0:2, sl, :], SC[0:2, sl, :], mrsb[0:2, 1, :]), inc="dve")
                    xa_act = S.emit("act", lambda h, dst=dst, sl=sl: h.mul(
                        xa[:, dst * c.TC:(dst + 1) * c.TC], SC[0:2, sl, :], 0.5),
                        waits=[("dve", d_)], inc="act")


            if stop_after <= 5:
                finish_early()
                nc.finalize()
                return nc
            with nc.psum_tensor(f"q1a{ch}", [128, c.TC], F32) as q1a, \
                 nc.psum_tensor(f"q1b{ch}", [128, c.TC], F32) as q1b, \
                 nc.psum_tensor(f"q1c{ch}", [128, c.TC], F32) as q1c, \
                 nc.psum_tensor(f"q3a{ch}", [128, c.TC], F32) as q3a, \
                 nc.psum_tensor(f"q3b{ch}", [128, c.TC], F32) as q3b, \
                 nc.psum_tensor(f"q3c{ch}", [128, c.TC], F32) as q3c, \
                 nc.psum_tensor(f"xa2p{ch}", [2, c.TC], F32) as xa2p:
                q1ps = [q1a, q1b, q1c]
                q3ps = [q3a, q3b, q3c]
                NBQ = (c.ITI + QF - 1) // QF
                wq_ld = [0] * NBQ
                q_done = [0] * c.ITI
                h_done = [0] * c.ITI
                xa2_done = [0] * c.ITI

                def emit_wq(b, ww):
                    it0 = b * QF
                    nits = min(QF, c.ITI - it0)
                    dw = ("dwa", "dwb", "dxt")[b % 3]
                    wq_ld[b] = S.emit("pool", lambda h, it0=it0, nits=nits, b=b: h.dma_start(
                        wpan3[:, b % 3, :nits * W13B].rearrange("p (i f) -> p i f", i=nits),
                        w13[it0:it0 + nits].rearrange("i p f -> p i f")), waits=ww, inc=dw, n=16)

                emit_wq(0, [("pe", stats_pe)])
                if NBQ > 1:
                    emit_wq(1, [("pe", stats_pe)])
                if NBQ > 2:
                    emit_wq(2, [("pe", stats_pe)])
                for it in range(c.ITI):
                    b = it // QF
                    bo = it - b * QF
                    dw = ("dwa", "dwb", "dxt")[b % 3]
                    i2 = it % 3

                    def qmm(h, base, ps, xap, i2=i2, b=b):
                        last = None
                        for kc in range(c.KDF):
                            last = h.matmul(ps[i2][:], wpan3[:, b % 3, base + kc * 128: base + (kc + 1) * 128],
                                            xfbf(kc), start=(kc == 0), stop=False)
                        last = h.matmul(ps[i2][:], wpan3[0:2, b % 3, base + c.KDF * 128: base + (c.KDF + 1) * 128],
                                        xap, start=False, stop=True)
                        return last
                    qw = [(dw, wq_ld[b]), ("act", xa_act), ("dve", xf_dve)]
                    if it >= 3:
                        qw.append(("dve", h_done[it - 3]))
                    S.emit("pe", lambda h, qmm=qmm, bo=bo: qmm(h, bo * W13B, q1ps, xa[:, 0:c.TC]),
                           waits=qw, inc="pe")
                    q_done[it] = S.emit("pe", lambda h, qmm=qmm, bo=bo: qmm(
                        h, bo * W13B + W1B, q3ps, xa[:, c.TC:2 * c.TC]), inc="pe")

                    aw = [("pe", q_done[it])]
                    if it >= 2:
                        aw.append(("dve", h_done[it - 2]))
                    sq1 = S.emit("act", lambda h, i2=i2, s2=it % 2: h.activation(
                        SC[:, s2, :], q1ps[i2][:], AF.Silu), waits=aw, inc="act")
                    hw = [("act", sq1)]
                    if it == 0:
                        hw.append(("pe", stats_pe))
                    if it >= 2:
                        hw.append(("pe", xa2_done[it - 2]))
                    h_done[it] = S.emit("dve", lambda h, it=it, i2=i2, s2=it % 2: h.tensor_tensor(
                        arena[:, it * c.TC:(it + 1) * c.TC], SC[:, s2, :], q3ps[i2][:],
                        op=ALU.mult), waits=hw, inc="dve")
                    def xa2_mm(ix):
                        xa2_done[ix] = S.emit("pe", lambda h, ix=ix: h.matmul(
                            xa2p[:], a2sb[:, ix * 2:(ix + 1) * 2], arena[:, ix * c.TC:(ix + 1) * c.TC],
                            start=(ix == 0), stop=(ix == c.ITI - 1)),
                            waits=[("dve", h_done[ix])], inc="pe")
                    if it >= 1:
                        xa2_mm(it - 1)
                    if it == c.ITI - 1:
                        xa2_mm(it)
                    if bo == min(QF, c.ITI - b * QF) - 1 and b + 3 < NBQ:
                        emit_wq(b + 3, [("pe", q_done[it])])
                S.emit("act", lambda h: h.mul(xa[:, 2 * c.TC:3 * c.TC], xa2p[:], 0.5),
                       waits=[("pe", xa2_done[c.ITI - 1])], inc="act")
                xa2_act = S.cnt["act"]
                hq_dve = S.cnt["dve"]

            if stop_after <= 6:
                finish_early()
                nc.finalize()
                return nc
            by_j = [[] for _ in range(c.KIF)]
            for it in range(c.ITI):
                for (p0, run, j, off) in fold_segments(it, c.IF):
                    by_j[j].append((it, p0, run, off))
            with nc.psum_tensor(f"hfp0{ch}", [128, c.TC], F32) as hfp0, \
                 nc.psum_tensor(f"hfp1{ch}", [128, c.TC], F32) as hfp1:
                hfp = [hfp0, hfp1]
                hfev = [0, 0]
                for j in range(c.KIF):
                    def fmm(h, j=j):
                        last = None
                        for q, (it, p0, run, off) in enumerate(by_j[j]):
                            if run == 128:
                                E = ident
                            elif p0 == 0 and off == 0:
                                E = idlow
                            elif p0 == 0 and off == 64:
                                E = eplus
                            elif p0 == 64 and off == 0:
                                E = eminus
                            else:
                                raise AssertionError((p0, run, off))
                            last = h.matmul(hfp[j % 2][:], E[:],
                                            arena[:, it * c.TC:(it + 1) * c.TC],
                                            start=(q == 0), stop=(q == len(by_j[j]) - 1))
                        return last
                    fw = [("dve", hq_dve), ("act", xa2_act)]
                    if j >= 2:
                        fw.append(("dve", hfev[j % 2]))
                    fpe = S.emit("pe", fmm, waits=fw, inc="pe")
                    hfev[j % 2] = S.emit("dve", lambda h, j=j: h.tensor_copy(
                        arena[:, j * c.TC:(j + 1) * c.TC], hfp[j % 2][:]),
                        waits=[("pe", fpe)], inc="dve")
                hf_dve = S.cnt["dve"]
                fold_pe_last = S.cnt["pe"]

            if stop_after <= 7:
                finish_early()
                nc.finalize()
                return nc
            with nc.psum_tensor(f"o2a{ch}", [128, c.TC], F32) as o2a, \
                 nc.psum_tensor(f"o2b{ch}", [128, c.TC], F32) as o2b, \
                 nc.psum_tensor(f"ssum2{ch}", [1, c.TC], F32) as ssum2, \
                 nc.psum_tensor(f"ssq2{ch}", [1, c.TC], F32) as ssq2, \
                 nc.psum_tensor(f"mbc2{ch}", [128, c.TC], F32) as mbc2, \
                 nc.psum_tensor(f"rbc2{ch}", [128, c.TC], F32) as rbc2:
                o2ps = [o2a, o2b]
                w2ld = [0] * c.HT
                o2_done = [0] * c.HT
                r2_done = [0] * c.HT
                sq2_done = [0] * c.HT
                st2_done = [0] * c.HT
                x2in = [0] * c.HT
                r2sp = [0] * c.HT
                for ht in range(c.HT):
                    h2 = ht % 2
                    dw = "dwa" if h2 == 0 else "dwb"
                    dsx = "dsa" if h2 == 0 else "dsb"
                    ww = [("pe", S.cnt["pe"])] if ht < 2 else [("pe", o2_done[ht - 2])]
                    if ht < 2:
                        ww = [("pe", xa2_done[c.ITI - 1])]
                    w2ld[ht] = S.emit("pool", lambda h, ht=ht, h2=h2: h.dma_start(
                        wpan[:, h2, :(c.KIF + 1) * 128], w2[ht]), waits=ww, inc=dw, n=16)
                    dsp = "dspa" if h2 == 0 else "dspb"
                    xw = list(spill_all) + [("dve", xf_dve), ("act", xa_act)]
                    if ht >= 2:
                        xw = [("dve", r2_done[ht - 2])]
                    x2in[ht] = S.emit("sp", lambda h, ht=ht, h2=h2: h.dma_start(
                        SA[:, h2, :], rscr[ht]), waits=xw, inc=dsx, n=16)

                    def o2mm(h, ht=ht, h2=h2):
                        last = None
                        for kc in range(c.KIF):
                            last = h.matmul(o2ps[h2][:], wpan[:, h2, kc * 128:(kc + 1) * 128],
                                            arena[:, kc * c.TC:(kc + 1) * c.TC],
                                            start=(kc == 0), stop=False)
                        last = h.matmul(o2ps[h2][:], wpan[0:2, h2, c.KIF * 128:(c.KIF + 1) * 128],
                                        xa[:, 2 * c.TC:3 * c.TC], start=False, stop=True)
                        return last
                    ow = [(dw, w2ld[ht]), ("dve", hf_dve), ("act", xa2_act)]
                    if ht >= 2:
                        ow.append(("dve", r2_done[ht - 2]))
                    o2_done[ht] = S.emit("pe", o2mm, waits=ow, inc="pe")

                    rw = [("pe", o2_done[ht]), (dsx, x2in[ht])]
                    if ht >= 2:
                        rw += [(dsp, r2sp[ht - 2]), ("pe", st2_done[ht - 2])]
                    S.emit("dve", lambda h, h2=h2: h.tensor_sub(
                        SC[:, h2, :], SA[:, h2, :], mrsb[:, 0, :]), waits=rw, inc="dve")
                    S.emit("dve", lambda h, h2=h2: h.tensor_mul(
                        SC[:, h2, :], SC[:, h2, :], mrsb[:, 1, :]), inc="dve")
                    r2_done[ht] = S.emit("dve", lambda h, h2=h2: h.tensor_tensor(
                        SB[:, h2, :], SC[:, h2, :], o2ps[h2][:], op=ALU.add),
                        waits=[], inc="dve")
                    qw = [("dve", r2_done[ht])]
                    if ht >= 2:
                        qw.append(("pe", st2_done[ht - 2]))
                    sq2_done[ht] = S.emit("act", lambda h, h2=h2: h.activation(
                        SC[:, h2, :], SB[:, h2, :], AF.Square), waits=qw, inc="act")
                    def stats2_mm(hs):
                        S.emit("pe", lambda h, hs=hs: h.matmul(
                            ssum2[:], onec[:], SB[:, hs % 2, :], start=(hs == 0), stop=(hs == c.HT - 1)),
                            waits=[("dve", r2_done[hs])], inc="pe")
                        st2_done[hs] = S.emit("pe", lambda h, hs=hs: h.matmul(
                            ssq2[:], onec[:], SC[:, hs % 2, :], start=(hs == 0), stop=(hs == c.HT - 1)),
                            waits=[("act", sq2_done[hs])], inc="pe")
                    if ht >= 1:
                        stats2_mm(ht - 1)
                    if ht == c.HT - 1:
                        stats2_mm(ht)
                    r2sp[ht] = S.emit("act", lambda h, ht=ht, h2=h2: h.dma_start(
                        rscr[ht], SB[:, h2, :]), waits=[("dve", r2_done[ht])], inc=dsp, n=16)
                stats2_pe = st2_done[c.HT - 1]
                spill2_all = [("dspa", S.cnt["dspa"]), ("dspb", S.cnt["dspb"])]

                S.emit("dve", lambda h: h.tensor_scalar_mul(meanb[:], ssum2[:], invD),
                       waits=[("pe", stats2_pe)], inc="dve")
                S.emit("dve", lambda h: h.tensor_scalar_mul(stA[:], ssq2[:], invD), inc="dve")
                S.emit("dve", lambda h: h.tensor_mul(stB[:], meanb[:], meanb[:]), inc="dve")
                S.emit("dve", lambda h: h.tensor_sub(stA[:], stA[:], stB[:]), inc="dve")
                S.emit("dve", lambda h: h.tensor_scalar_add(stA[:], stA[:], 1e-5), inc="dve")
                var_dve = S.cnt["dve"]
                S.emit("act", lambda h: h.activation(stB[:], stA[:], AF.Sqrt),
                       waits=[("dve", var_dve)], inc="act")
                S.emit("dve", lambda h: h.reciprocal(rstdb[:], stB[:]),
                       waits=[("act", S.cnt["act"])], inc="dve")
                ln2_dve = S.cnt["dve"]

            if True:
                S.emit("pe", lambda h: h.matmul(mbc2[:], oner[:], meanb[:], start=True, stop=True),
                       waits=[("dve", ln2_dve), ("act", S.cnt["act"])], inc="pe")
                bc2_pe = S.emit("pe", lambda h: h.matmul(rbc2[:], oner[:], rstdb[:], start=True, stop=True),
                                inc="pe")
                S.emit("dve", lambda h: h.tensor_copy(mrsb[:, 0, :], mbc2[:]),
                       waits=[("pe", bc2_pe), ("act", S.cnt["act"])], inc="dve")
                bc2_cp = S.emit("dve", lambda h: h.tensor_copy(mrsb[:, 1, :], rbc2[:]), inc="dve")

                out_done = [0] * c.HT
                outst = [0] * c.HT
                for dt in range(c.HT):
                    h2 = dt % 2
                    dsx = "dsa" if h2 == 0 else "dsb"
                    dou = "douta" if h2 == 0 else "doutb"
                    xw = list(spill2_all)
                    if dt >= 2:
                        xw = [("dve", out_done[dt - 2])]
                    rin2 = S.emit("sp", lambda h, dt=dt, h2=h2: h.dma_start(
                        SA[:, h2, :], rscr[dt]), waits=xw, inc=dsx, n=16)
                    tw = [(dsx, rin2), ("dve", bc2_cp)]
                    if dt >= 2:
                        tw.append((dou, outst[dt - 2]))
                    S.emit("dve", lambda h, h2=h2: h.tensor_sub(SC[:, h2, :], SA[:, h2, :], mrsb[:, 0, :]),
                           waits=tw, inc="dve")
                    out_done[dt] = S.emit("dve", lambda h, h2=h2: h.tensor_mul(
                        SB[:, h2, :], SC[:, h2, :], mrsb[:, 1, :]), inc="dve")
                    outst[dt] = S.emit("act", lambda h, dt=dt, h2=h2, t0=t0: h.dma_start(
                        outT[dt * 128:(dt + 1) * 128, t0:t0 + c.TC], SB[:, h2, :]),
                        waits=[("dve", out_done[dt])], inc=dou, n=16)

            pe_end_prev_chunk = S.cnt["pe"]
            dve_end_prev_chunk = S.cnt["dve"]
            xtb_pe_prev = o2_done[c.HT - 1]
            xtb_dve_prev = xf_dve
            arena_pe_prev = o2_done[c.HT - 1]
            ln2psum_prev = bc2_cp

        S.emit("sp", lambda h: h.nop(),
               waits=[("douta", S.cnt["douta"]), ("doutb", S.cnt["doutb"])])
        S.emit("act", lambda h: h.nop(),
               waits=[("douta", S.cnt["douta"]), ("doutb", S.cnt["doutb"])])
        S.emit("pool", lambda h: h.nop(),
               waits=[("douta", S.cnt["douta"]), ("doutb", S.cnt["doutb"])])

        finish()

    nc.finalize()
    return nc



def host_layout(c, L0_attn_w_gate, L0_attn_w_up, L0_attn_w_down,
                L0_ffn_w1q, L0_ffn_b1, L0_ffn_w3q, L0_ffn_b3,
                L0_ffn_w2q, L0_ffn_b2, L0_ffn_a1, L0_ffn_a3, L0_ffn_a2):
    ac = np.ascontiguousarray
    f = np.float32

    def gu_panels(w):
        A = w.reshape(c.NPAN, c.PW, c.KD, 128)
        return ac(A.transpose(0, 3, 2, 1).reshape(c.NPAN, 128, c.KD * c.PW), dtype=f)

    wg_h = gu_panels(L0_attn_w_gate)
    wu_h = gu_panels(L0_attn_w_up)
    B = L0_attn_w_down.reshape(c.HT, 128, c.ITI, 128)
    wd_h = ac(B.transpose(0, 3, 2, 1).reshape(c.HT, 128, c.ITI * 128), dtype=f)

    def w13(wq, b):
        out = np.zeros((c.ITI, 128, (c.KDF + 1) * 128), dtype=f)
        A = wq.reshape(c.ITI, 128, c.KDF, 128).transpose(0, 3, 2, 1)
        out[:, :, :c.KDF * 128] = A.reshape(c.ITI, 128, c.KDF * 128)
        bt = b.reshape(c.ITI, 128, 2).transpose(0, 2, 1)
        out[:, 0:2, c.KDF * 128:] = bt
        return ac(out)

    w1_h = w13(L0_ffn_w1q, L0_ffn_b1)
    w3_h = w13(L0_ffn_w3q, L0_ffn_b3)
    w13_h = ac(np.concatenate([w1_h, w3_h], axis=2))

    w2p = np.zeros((c.D, c.KIF * 128), dtype=f)
    w2p[:, :c.IF] = L0_ffn_w2q
    out2 = np.zeros((c.HT, 128, (c.KIF + 1) * 128), dtype=f)
    A2w = w2p.reshape(c.HT, 128, c.KIF, 128).transpose(0, 3, 2, 1)
    out2[:, :, :c.KIF * 128] = A2w.reshape(c.HT, 128, c.KIF * 128)
    out2[:, 0:2, c.KIF * 128:] = L0_ffn_b2.reshape(c.HT, 128, 2).transpose(0, 2, 1)
    w2_h = ac(out2)

    a1_h = ac(L0_ffn_a1.T.reshape(c.KD, 128, 2), dtype=f)
    a3_h = ac(L0_ffn_a3.T.reshape(c.KD, 128, 2), dtype=f)
    a2_h = ac(L0_ffn_a2.T.reshape(c.ITI, 128, 2), dtype=f)
    return dict(wg=wg_h, wu=wu_h, wd=wd_h, w13=w13_h, w2=w2_h,
                a1=a1_h, a3=a3_h, a2=a2_h)


_FULL = Cfg(D=4096, I=11008, T=1024, PW=344)
N_CORES = 8


def kernel(L0_attn_x, L0_attn_w_gate, L0_attn_w_up, L0_attn_w_down,
           L0_ffn_w1q, L0_ffn_a1, L0_ffn_b1, L0_ffn_w3q, L0_ffn_a3,
           L0_ffn_b3, L0_ffn_w2q, L0_ffn_a2, L0_ffn_b2):
    c = _FULL
    x = np.asarray(L0_attn_x, dtype=np.float32)
    B_, Sq, D = x.shape
    NT = B_ * Sq
    xt = np.ascontiguousarray(x.reshape(NT, D).T)
    weights = host_layout(
        c,
        np.asarray(L0_attn_w_gate, np.float32), np.asarray(L0_attn_w_up, np.float32),
        np.asarray(L0_attn_w_down, np.float32),
        np.asarray(L0_ffn_w1q, np.float32), np.asarray(L0_ffn_b1, np.float32),
        np.asarray(L0_ffn_w3q, np.float32), np.asarray(L0_ffn_b3, np.float32),
        np.asarray(L0_ffn_w2q, np.float32), np.asarray(L0_ffn_b2, np.float32),
        np.asarray(L0_ffn_a1, np.float32), np.asarray(L0_ffn_a3, np.float32),
        np.asarray(L0_ffn_a2, np.float32))

    nc = build_program(c)
    in_maps = []
    for core in range(N_CORES):
        m = dict(weights)
        m["xT"] = np.ascontiguousarray(xt[:, core * c.T:(core + 1) * c.T])
        in_maps.append(m)
    res = run_bass_kernel_spmd(nc, in_maps, core_ids=list(range(N_CORES)))
    outT = np.concatenate([res.results[i]["outT"] for i in range(N_CORES)], axis=1)
    out = np.ascontiguousarray(outT.T).reshape(B_, Sq, D).astype(np.float32)
    return out
